# version 08
# baseline (speedup 1.0000x reference)
import numpy as np
import concourse.bass as bass
import concourse.mybir as mybir

F32 = mybir.dt.float32
BF16 = mybir.dt.bfloat16
AF = mybir.ActivationFunctionType
ALU = mybir.AluOpType
AX = mybir.AxisListType

T, D, H, DK, M, INTER = 4096, 1024, 16, 64, 256, 4096
NT, TS, NSUB = 8, 512, 4
EPS = 1e-6
EPS16 = EPS / 16.0
LN16 = float(np.log(16.0))
LNEPS = 1e-5
ENGS = ("tensor", "vector", "scalar", "sync", "gpsimd")


def build_program(debug=False):
    nc = bass.Bass("TRN2", target_bir_lowering=False)
    din = lambda n, s, dt=F32: nc.dram_tensor(n, s, dt, kind="ExternalInput").ap()
    dout = lambda n, s, dt=F32: nc.dram_tensor(n, s, dt, kind="ExternalOutput").ap()

    xT = din("xT", [D, T])
    x_tm = din("x", [T, D])
    WqT_h = din("WqT", [D, D])
    WkT_h = din("WkT", [D, D])
    WvT_h = din("WvT", [D, D])
    WoT_h = din("WoT", [D, D])
    W1_h = din("W1", [D, INTER])
    W2_h = din("W2", [INTER, D])
    projP = din("projP", [8, 128, M])
    ind_h = din("ind", [8, 128, H])
    ident_h = din("ident", [128, 128])
    onesrow_h = din("onesrow", [1, 128])
    onescol_h = din("onescol", [128, 1])
    borow_h = din("borow", [1, D])
    b2row_h = din("b2row", [1, D])
    b1col_h = din("b1col", [128, 32])
    out_h = dout("out", [T, D])
    dbg = {}
    if debug:
        dbg["kv"] = dout("dbg_kv", [128, 2048])
        dbg["ksum"] = dout("dbg_ksum", [128, 32])
        dbg["phik"] = dout("dbg_phik", [128, H * M])
        dbg["r1n"] = dout("dbg_r1n", [T, D])

    ctx = []

    def sb(name, shape, dt):
        h = nc.sbuf_tensor(name, shape, dt)
        t_ = h.__enter__()
        ctx.append(h)
        return t_

    def psum(name, shape, dt):
        h = nc.psum_tensor(name, shape, dt)
        t_ = h.__enter__()
        ctx.append(h)
        return t_

    mega = sb("mega", [128, 64 * 1024], BF16)
    W1V = mega[:, 0:32768].rearrange("p (c e) -> p c e", c=8)
    W2V = mega[:, 32768:65536].rearrange("p (c e) -> p c e", c=32)
    WA = mega[:, 0:8192].rearrange("p (c e) -> p c e", c=8)
    WB = mega[:, 8192:16384].rearrange("p (c e) -> p c e", c=8)
    XB = mega[:, 16384:24576].rearrange("p (b c t) -> p b c t", b=2, c=8)
    KT = mega[:, 24576:28672].rearrange("p (c t) -> p c t", c=8)
    KSQ = mega[:, 28672:32768].rearrange("p (c t) -> p c t", c=8)
    APR = mega[:, 32768:36864].rearrange("p (h m) -> p h m", h=H)
    PHI0 = mega[:, 36864:40960].rearrange("p (h m) -> p h m", h=H)
    PHI1 = mega[:, 47648:51744].rearrange("p (h m) -> p h m", h=H)
    PHIB = [PHI0, PHI1]
    VAUG = mega[:, 40960:42016].rearrange("p (h j) -> p h j", h=H)
    MRG = mega[:, 42016:43040]
    PQT = mega[:, 43040:43552].rearrange("p (b m) -> p b m", b=2)
    PQT8 = mega[:, 45600:47648]
    MGT = mega[:, 43552:44576].rearrange("p (c t) -> p c t", c=8)
    JNK = mega[:, 44576:45600]

    fpool = sb("fpool", [128, 7552], F32)
    KVACC = fpool[:, 0:2080]
    NSQ = fpool[:, 2112:2176].rearrange("p (s h) -> p s h", s=4)
    BIA = fpool[:, 2176:2240].rearrange("p (s h) -> p s h", s=4)
    ENSQ = fpool[:, 2240:2304].rearrange("p (s h) -> p s h", s=4)
    MAXA = fpool[:, 2304:2368].rearrange("p (s h) -> p s h", s=4)
    SFT = fpool[:, 2368:2432].rearrange("p (s h) -> p s h", s=4)
    DEN = fpool[:, 2432:2448]
    RDEN = fpool[:, 2448:2464]
    R1 = fpool[:, 2464:3488]
    XTM = fpool[:, 3488:4512]
    MU = fpool[:, 4512:4520]
    OUTS = fpool[:, 4520:6568].rearrange("p (b e) -> p b e", b=2)

    r1n = sb("r1n", [128, 16, 1024], BF16)
    KVR = sb("kvr", [128, 16, 2, 66], BF16)
    R1NT = sb("r1nt", [128, 8, 256], BF16)
    USB = sb("usb", [128, 2, 256], BF16)
    IDN = sb("idn", [128, 128], BF16)
    IND = sb("indb", [128, 8, 16], BF16)
    PRJ = sb("prj", [128, 8, 256], BF16)
    ONESR = sb("onesr", [1, 128], BF16)
    ONESC = sb("onesc", [128, 1], BF16)
    BOROW = sb("borow_sb", [1, 1024], BF16)
    B2ROW = sb("b2row_sb", [1, 1024], BF16)
    B1COL = sb("b1col_sb", [128, 32], F32)

    pb = [psum(f"pb{i}", [128, 512], F32) for i in range(6)]
    pbT = psum("pbT", [128, 1024], BF16)
    pb6_cm = nc.psum_tensor("pb6", [128, 512], F32)
    pb6 = pb6_cm.__enter__()
    KVB = [pb[3], pb[4], pb[5], pb6]
    NUMB = [pb[2], pb[3]]
    YB = [pb[4], pb[5]]
    VB = [pb[2], pb[3], pb[4], pb[5]]

    sems = {}
    cnt = {}
    for s_ in ("ldw", "ldx", "ldy", "st", "t2v", "t2s", "v2t", "s2v"):
        h = nc.semaphore(s_)
        sems[s_] = h.__enter__()
        ctx.append(h)
        cnt[s_] = 0
    ops = {e: [] for e in ENGS}

    class Rec:
        def __init__(self, eng, auto=None):
            self.e = eng
            self.auto = auto

        def wait(self, sem, v):
            if v > 0:
                ops[self.e].append(lambda g, s=sems[sem], v=v: g.wait_ge(s, v))

        def op(self, meth, *a, inc=(), **kw):
            inc = list(inc)
            if self.auto is not None and meth != "dma_start" and not inc:
                inc = [(self.auto, 1)]
            assert len(inc) <= 1, inc
            incs = []
            for sname, n in inc:
                cnt[sname] += n
                incs.append((sems[sname], n))

            def f(g, meth=meth, a=a, kw=kw, incs=incs):
                r = getattr(g, meth)(*a, **kw)
                for s_, n_ in incs:
                    r = r.then_inc(s_, n_)
            ops[self.e].append(f)

    t = Rec("tensor")
    v = Rec("vector", auto="v2t")
    s = Rec("scalar", auto="s2v")
    sy = Rec("sync")
    g = Rec("gpsimd")
    val = lambda sem: cnt[sem]

    def dma(rec, sem, **kw):
        rec.wait(sem, val(sem))
        rec.op("dma_start", inc=[(sem, 16)], **kw)

    def vsync():
        v.wait("v2t", val("v2t"))

    dma(g, "ldw", out=IDN[:, :], in_=ident_h[:, :])
    dma(g, "ldw", out=IND[:, :, :], in_=ind_h.rearrange("c p h -> p c h"))
    dma(g, "ldw", out=PRJ[:, :, :], in_=projP.rearrange("c p m -> p c m"))
    dma(g, "ldw", out=ONESR[:, :], in_=onesrow_h[:, :])
    dma(g, "ldw", out=ONESC[:, :], in_=onescol_h[:, :])
    dma(g, "ldw", out=BOROW[:, :], in_=borow_h[:, :])
    dma(g, "ldw", out=B2ROW[:, :], in_=b2row_h[:, :])
    dma(g, "ldw", out=B1COL[:, :], in_=b1col_h[:, :])
    dma(g, "ldw", out=WA[:, :, :], in_=WkT_h.rearrange("(c p) e -> p c e", p=128))
    dma(g, "ldw", out=WB[:, :, :], in_=WvT_h.rearrange("(c p) e -> p c e", p=128))
    ldw_p1 = val("ldw")

    v.op("memset", KVACC[:, :], 0.0)
    v.wait("ldw", 16 * 5)
    v.op("memset", VAUG[:, :, 65:66], 0.0)
    v.op("memset", VAUG[:, :, 64:65], 1.0)
    vaug_ones = val("v2t")
    kvacc_zeroed = val("v2t")

    st_ = dict(
        kproj_free=[0, 0],
        xw_free=[0, 0],
        nsq_bank_free=0,
        kv_banks_free=0,
        kvacc_done=kvacc_zeroed,
        vaug_ready=0,
        phi_free=0,
        phi_free2=[0, 0],
        bia_ready=0,
        ensq_ready=0,
        nsq_evacd=0,
        apr_last=0,
        num_free=0,
        pqt_free=0,
        pqt_free2w=[0, 0],
        pbT_free=[0, 0],
        mrg_free=0,
        mgt_free=0,
        y_free=0,
        kvr_ready=0,
        xtm_free={},
        u_free=[0, 0],
        usb_free=[0, 0],
        r1nt_free=0,
        v_free=0,
        outs_free={},
        xb_ev={},
        xb_rel={},
    )

    def load_xb(tile, phase):
        buf = tile % 2
        key = None
        if phase == "p1" and tile >= 2:
            key = ("p1", tile - 2)
        elif phase == "p2":
            key = ("p1", 6 + tile) if tile < 2 else ("p2", tile - 2)
        if key is not None:
            sem, vv = st_["xb_rel"][key]
            g.wait(sem, vv)
        dma(g, "ldx", out=XB[:, buf, :, :],
            in_=xT.rearrange("(c p) t -> p c t", p=128)[:, :, tile * TS:(tile + 1) * TS])
        st_["xb_ev"][(phase, tile)] = val("ldx")

    def proj_nsq(tile, phase, ldw_cur):
        buf = tile % 2
        t.wait("ldx", st_["xb_ev"][(phase, tile)])
        t.wait("ldw", ldw_cur)
        for c in range(8):
            bank = c % 2
            t.wait("v2t", st_["kproj_free"][bank])
            t.wait("s2v", st_["xw_free"][bank])
            for d in range(8):
                t.op("matmul", pb[bank][:, :], WA[:, d, c * 128:(c + 1) * 128],
                     XB[:, buf, d, :], start=(d == 0), stop=(d == 7),
                     inc=[("t2v", 1)] if d == 7 else ())
            ev = val("t2v")
            if phase == "p2" and c == 7:
                st_["xb_rel"][(phase, tile)] = ("t2v", ev)
            v.wait("t2v", ev)
            v.op("tensor_copy", KT[:, c, :], pb[bank][:, :])
            st_["kproj_free"][bank] = val("v2t")
            vsync()
            v.op("tensor_mul", KSQ[:, c, :], KT[:, c, :], KT[:, c, :])
        ksq_ready = val("v2t")
        t.wait("v2t", ksq_ready)
        t.wait("v2t", st_["nsq_bank_free"])
        t.wait("v2t", st_["num_free"])
        first = True
        for su in range(4):
            for c in range(8):
                last = (su == 3 and c == 7)
                t.op("matmul", pb[2][:, 32 + su * 16:48 + su * 16],
                     KSQ[:, c, su * 128:(su + 1) * 128], IND[:, c, :],
                     start=first, stop=last,
                     inc=[("t2v", 1)] if last else ())
                first = False
        nsq_mm = val("t2v")
        v.wait("t2v", nsq_mm)
        v.op("tensor_copy", NSQ[:, :, :],
             pb[2].rearrange("p (a b) -> p a b", b=16)[:, 2:6, :])
        st_["nsq_evacd"] = val("v2t")
        v.wait("s2v", max(st_["ensq_ready"], st_["apr_last"]))
        vsync()
        v.op("tensor_scalar", out=BIA[:, :, :].rearrange("p s h -> p (s h)"),
             in0=NSQ[:, :, :].rearrange("p s h -> p (s h)"), scalar1=-0.5 / 64.0,
             scalar2=-LN16, op0=ALU.mult, op1=ALU.add)
        st_["bia_ready"] = val("v2t")
        s.wait("v2t", st_["bia_ready"])
        s.op("activation", out=ENSQ[:, :, :].rearrange("p s h -> p (s h)"),
             in_=BIA[:, :, :].rearrange("p s h -> p (s h)"), func=AF.Exp)
        st_["ensq_ready"] = val("s2v")

    def emit_xw_pair(su, p):
        t.wait("s2v", max(st_["xw_free"]))
        t.wait("v2t", max(st_["kproj_free"]))
        lo = KT[0:64, p, su * 128:(su + 1) * 128]
        hi = KT[64:128, p, su * 128:(su + 1) * 128]
        t.op("matmul", pb[0][:, 0:256], lo, PRJ[0:64, p, :],
             start=True, stop=True, tile_position=(0, 0))
        t.op("matmul", pb[1][:, 0:256], hi, PRJ[64:128, p, :],
             start=True, stop=True, tile_position=(64, 0), inc=[("t2s", 1)])
        xw_ev = val("t2s")
        s.wait("t2s", xw_ev)
        s.wait("v2t", max(st_["bia_ready"], st_.get("apr_free_v2t", 0)))
        for h2 in range(2):
            h = 2 * p + h2
            s.op("activation", out=APR[:, h, :],
                 in_=pb[h2][:, 0:256],
                 func=AF.Exp, bias=BIA[:, su, h:h + 1], scale=0.125)
        st_["xw_free"] = [val("s2v"), val("s2v")]
        if p == 7:
            st_["apr_last"] = val("s2v")

        PHI = PHIB[su % 2]
        pst = st_.setdefault("phipipe", {})

        def stage_reduce(q):
            sl = slice(2 * q, 2 * q + 2)
            v.wait("s2v", pst[("exps", su, q)])
            if q == 0:
                v.wait("s2v", st_["ensq_ready"])
                v.wait("t2v", st_["phi_free2"][su % 2])
            v.op("reduce_max", MAXA[:, su, sl], APR[:, sl, :], axis=AX.X)
            pst[("red", su, q)] = val("v2t")

        def stage_rest(q):
            sl = slice(2 * q, 2 * q + 2)
            v.wait("v2t", pst[("red", su, q)])
            v.op("reciprocal", SFT[:, su, sl], MAXA[:, su, sl])
            rc = val("v2t")
            v.wait("v2t", rc)
            v.op("tensor_mul", SFT[:, su, sl], SFT[:, su, sl], ENSQ[:, su, sl])
            mv = val("v2t")
            v.wait("v2t", mv)
            for h in (2 * q, 2 * q + 1):
                v.op("tensor_scalar", out=PHI[:, h, :], in0=APR[:, h, :],
                     scalar1=SFT[:, su, h:h + 1], scalar2=EPS16,
                     op0=ALU.mult, op1=ALU.add)

        pst[("exps", su, p)] = val("s2v")
        stage_reduce(p)
        if p > 0:
            stage_rest(p - 1)
        if p == 7:
            stage_rest(7)
            st_["apr_free_v2t"] = val("v2t")
        return val("v2t")

    def emit_phi_vec(su):
        return val("v2t")

    def emit_kv(su, h0, h1, phi_ready_su):
        firstk = (su == 0)
        t.wait("v2t", phi_ready_su)
        if h0 == 0:
            t.wait("v2t", st_["vaug_ready_kv"])
            if su == 0:
                t.wait("v2t", st_["kv_banks_free"])
                t.wait("v2t", st_["nsq_evacd"])
                t.wait("v2t", st_["nsq_bank_free"])
        PHI = PHIB[su % 2]
        for h in range(h0, h1):
            for c in range(2):
                idx = h * 2 + c
                if idx < 28:
                    bank, slot = KVB[idx // 7], idx % 7
                else:
                    bank, slot = pb[2], idx - 28
                stop_kv = (su == 3 and (slot == 6 or idx == 31))
                t.op("matmul", bank[:, slot * 65:slot * 65 + 65],
                     PHI[:, h, c * 128:(c + 1) * 128], VAUG[:, h, 0:65],
                     start=(firstk and slot == 0), stop=stop_kv,
                     inc=[("t2v", 1)] if (stop_kv or idx == 31) else ())
        if h1 == H:
            st_["phi_free2"][su % 2] = val("t2v")

    st_["vaug_ready_kv"] = 0
    for tile in range(NT):
        load_xb(tile, "p1")
        proj_nsq(tile, "p1", ldw_p1)
        prev_phi_ready = st_.get("p1_phi_ready", 0)
        for su in range(4):
            buf = tile % 2
            for p in range(8):
                ph = emit_xw_pair(su, p)
                if su > 0 and p < 4:
                    emit_kv(su - 1, p * 4, (p + 1) * 4, prev_phi_ready)
            cur_phi_ready = ph
            for gr in range(2):
                bank = gr
                t.wait("s2v", st_["xw_free"][bank])
                t.wait("v2t", st_["kproj_free"][bank])
                for d in range(8):
                    lastd = (d == 7)
                    t.op("matmul", pb[bank][:, :],
                         XB[:, buf, d, su * 128:(su + 1) * 128],
                         WB[:, d, gr * 512:(gr + 1) * 512],
                         start=(d == 0), stop=lastd,
                         inc=[("t2v", 1)] if lastd else ())
                ev = val("t2v")
                if su == 3 and gr == 1:
                    st_["xb_rel"][("p1", tile)] = ("t2v", ev)
                v.wait("t2v", ev)
                v.wait("t2v", st_["phi_free2"][(su - 1) % 2])
                if tile == 0 and su == 0 and gr == 0:
                    v.wait("v2t", vaug_ones)
                v.op("tensor_copy", VAUG[:, gr * 8:(gr + 1) * 8, 0:64],
                     pb[bank].rearrange("p (h e) -> p h e", h=8)[:, :, :])
                st_["kproj_free"][bank] = val("v2t")
            st_["vaug_ready_kv"] = val("v2t")
            prev_phi_ready = cur_phi_ready
            st_["p1_phi_ready"] = prev_phi_ready
        emit_kv(3, 0, H, prev_phi_ready)
        kv_done = val("t2v")
        v.wait("t2v", kv_done)
        v.wait("v2t", st_["kvacc_done"])
        for b in range(4):
            v.op("tensor_add", KVACC[:, b * 455:(b + 1) * 455],
                 KVACC[:, b * 455:(b + 1) * 455], KVB[b][:, 0:455])
        v.op("tensor_add", KVACC[:, 1820:2080], KVACC[:, 1820:2080], pb[2][:, 0:260])
        st_["kvacc_done"] = val("v2t")
        st_["kv_banks_free"] = val("v2t")
        st_["nsq_bank_free"] = val("v2t")

    vsync()
    v.op("tensor_copy", KVR[:, :, :, 0:65],
         KVACC.rearrange("p (h c e) -> p h c e", h=16, c=2))
    v.op("memset", KVR[:, :, :, 65:66], 0.0)
    st_["kvr_ready"] = val("v2t")
    if debug:
        vsync()
        v.op("tensor_copy", R1.rearrange("p (i j) -> p i j", j=64),
             KVACC.rearrange("p (i j) -> p i j", j=65)[:, 0:16, 0:64])
        vsync()
        sy.wait("v2t", val("v2t"))
        dma(sy, "st", out=dbg["kv"][:, 0:1024], in_=R1[:, :])
        v.wait("st", val("st"))
        v.op("tensor_copy", R1.rearrange("p (i j) -> p i j", j=64),
             KVACC.rearrange("p (i j) -> p i j", j=65)[:, 16:32, 0:64])
        vsync()
        sy.wait("v2t", val("v2t"))
        dma(sy, "st", out=dbg["kv"][:, 1024:2048], in_=R1[:, :])
        v.wait("st", val("st"))
        v.op("tensor_copy", R1[:, 0:32].rearrange("p (i j) -> p i j", j=1),
             KVACC.rearrange("p (i j) -> p i j", j=65)[:, :, 64:65])
        vsync()
        sy.wait("v2t", val("v2t"))
        dma(sy, "st", out=dbg["ksum"][:, :], in_=R1[:, 0:32])
        v.wait("st", val("st"))

    pb6_cm.__exit__(None, None, None)
    pbT2 = nc.psum_tensor("pbT2", [128, 1024], BF16)
    pbT2t = pbT2.__enter__()
    ctx.append(pbT2)
    pbTs = [pbT, pbT2t]

    def phase_reset():
        st_["kproj_free"] = [val("v2t"), val("v2t")]
        st_["xw_free"] = [val("s2v"), val("s2v")]
        st_["nsq_bank_free"] = val("v2t")
        st_["num_free"] = val("v2t")
        st_["y_free"] = val("v2t")
        st_["v_free"] = val("v2t")
        st_["mgt_free"] = val("t2v")
        st_["mrg_free"] = val("t2v")
        st_["pbT_free"] = [val("v2t"), val("v2t")]
        st_["pqt_free"] = val("t2v")
        st_["pqt_free2w"] = [val("t2v"), val("t2v")]
        st_["phi_free"] = val("t2v")
        st_["phi_free2"] = [val("t2v"), val("t2v")]
        st_["u_free"] = [val("s2v"), val("s2v")]
        st_["usb_free"] = [val("t2s"), val("t2s")]
        st_["r1nt_free"] = val("t2s")

    def phase_barrier():
        g.wait("t2v", val("t2v"))
        g.wait("t2s", val("t2s"))
        g.wait("v2t", val("v2t"))
        g.wait("s2v", val("s2v"))

    for half in range(2):
        phase_barrier()
        dma(g, "ldw", out=WA[:, :, :], in_=WqT_h.rearrange("(c p) e -> p c e", p=128))
        dma(g, "ldw", out=WB[:, :, :], in_=WoT_h.rearrange("(c p) e -> p c e", p=128))
        ldw_p2 = val("ldw")
        phase_reset()

        for tile in range(half * 4, half * 4 + 4):
            load_xb(tile, "p2")
            proj_nsq(tile, "p2", ldw_p2)
            pend = []
            for p in range(8):
                st_["p2_phi_ready"] = emit_xw_pair(0, p)
            for su in range(4):
                sub = tile * 4 + su
                rsub = sub % 16
                if sub >= 1:
                    sy.wait("v2t", st_["xtm_free"].get(sub - 1, 0))
                dma(sy, "ldy", out=XTM[:, :],
                    in_=x_tm[sub * 128:(sub + 1) * 128, :])
                xtm_ev = val("ldy")
                phi_ready = st_["p2_phi_ready"]
                pend = list(range(8)) if su < 3 else []

                def drip(k):
                    for _ in range(k):
                        if pend:
                            st_["p2_phi_ready"] = emit_xw_pair(su + 1, pend.pop(0))

                PHI = PHIB[su % 2]
                t.wait("v2t", st_["kvr_ready"])
                wave_trev = {}
                wave_evac = {}
                grp_start_done = [False, False]

                def emit_T(w):
                    bnk = w % 2
                    t.wait("v2t", phi_ready)
                    t.wait("v2t", st_["pbT_free"][bnk])
                    t.wait("v2t", st_["num_free"] if w < 2 else 0)
                    for hh4 in range(4):
                        h = w * 4 + hh4
                        for c in range(2):
                            slot = hh4 * 2 + c
                            t.op("transpose", pbTs[bnk][:, slot * 128:(slot + 1) * 128],
                                 PHI[:, h, c * 128:(c + 1) * 128], IDN[:, :],
                                 inc=[("t2v", 1)])
                    wave_trev[w] = val("t2v")
                    if w == 3:
                        st_["phi_free2"][su % 2] = wave_trev[w]

                def emit_evac(w):
                    bnk = w % 2
                    v.wait("t2v", wave_trev[w])
                    v.wait("t2v", st_["pqt_free2w"][w % 2])
                    v.op("tensor_copy", PQT8[:, (w % 2) * 1024:(w % 2) * 1024 + 1024],
                         pbTs[bnk][:, :])
                    st_["pbT_free"][bnk] = val("v2t")
                    wave_evac[w] = val("v2t")

                def emit_num(w):
                    grp = w // 2
                    t.wait("v2t", wave_evac[w])
                    if w % 2 == 0:
                        t.wait("v2t", st_["num_free"])
                    for hh4 in range(4):
                        hh = (w % 2) * 4 + hh4
                        h = w * 4 + hh4
                        nb = NUMB[w % 2]
                        nslot = hh4
                        base = (w % 2) * 1024 + hh4 * 256
                        for c in range(2):
                            t.op("matmul", nb[:, nslot * 128:nslot * 128 + 66],
                                 PQT8[:, base + c * 128:base + c * 128 + 128],
                                 KVR[:, h, c, :],
                                 start=(c == 0 and hh4 == 0),
                                 stop=(nslot == 3 and c == 1),
                                 inc=[("t2v", 1)] if c == 1 else ())
                    st_["pqt_free2w"][w % 2] = val("t2v")

                emit_T(0)
                emit_evac(0)
                for w in range(4):
                    if w < 3:
                        emit_T(w + 1)
                        emit_evac(w + 1)
                    emit_num(w)
                    drip(1)
                    if w % 2 == 1:
                        numg = val("t2v")
                        v.wait("t2v", numg)
                        base_h = (w - 1) * 4
                        for bb in range(2):
                            v.op("tensor_scalar",
                                 out=DEN[:, base_h + bb * 4:base_h + bb * 4 + 4],
                                 in0=NUMB[bb].rearrange("p (a b) -> p a b", b=128)[:, :, 64:65]
                                 .rearrange("p a b -> p (a b)"),
                                 scalar1=EPS, scalar2=None, op0=ALU.add)
                        vsync()
                        v.op("reciprocal", RDEN[:, base_h:base_h + 8],
                             DEN[:, base_h:base_h + 8])
                        vsync()
                        v.wait("t2v", st_["mrg_free"])
                        for hh in range(8):
                            h = base_h + hh
                            v.op("tensor_scalar", out=MRG[:, h * 64:(h + 1) * 64],
                                 in0=NUMB[hh // 4][:, (hh % 4) * 128:(hh % 4) * 128 + 64],
                                 scalar1=RDEN[:, h:h + 1], scalar2=None, op0=ALU.mult)
                        st_["num_free"] = val("v2t")
                        drip(1)
                mrg_ready = val("v2t")
                t.wait("v2t", mrg_ready)
                t.wait("v2t", st_["pbT_free"][0])
                for c in range(8):
                    t.op("transpose", pbTs[0][:, c * 128:(c + 1) * 128],
                         MRG[:, c * 128:(c + 1) * 128], IDN[:, :],
                         inc=[("t2v", 1)])
                st_["mrg_free"] = val("t2v")
                drip(2)
                v.wait("t2v", st_["mrg_free"])
                v.wait("t2v", st_["mgt_free"])
                v.op("tensor_copy", MGT[:, :, :], pbTs[0].rearrange("p (c t) -> p c t", c=8))
                st_["pbT_free"][0] = val("v2t")
                mgt_ready = val("v2t")
                t.wait("v2t", mgt_ready)
                t.wait("v2t", st_["y_free"])
                for gr in range(2):
                    for ic in range(8):
                        t.op("matmul", YB[gr][:, :], MGT[:, ic, :],
                             WB[:, ic, gr * 512:(gr + 1) * 512],
                             start=(ic == 0), stop=False)
                    t.op("matmul", YB[gr][:, :], ONESR[:, :],
                         BOROW[:, gr * 512:(gr + 1) * 512],
                         start=False, stop=True, inc=[("t2v", 1)])
                st_["mgt_free"] = val("t2v")
                yev = val("t2v")
                drip(8)
                v.wait("t2v", yev)
                v.wait("ldy", xtm_ev)
                vsync()
                for gr in range(2):
                    v.op("scalar_tensor_tensor", out=R1[:, gr * 512:(gr + 1) * 512],
                         in0=YB[gr][:, :], scalar=1.0,
                         in1=XTM[:, gr * 512:(gr + 1) * 512],
                         op0=ALU.mult, op1=ALU.add)
                st_["y_free"] = val("v2t")
                st_["xtm_free"][sub] = val("v2t")
                vsync()
                v.op("reduce_sum", MU[:, 0:1], R1[:, :], axis=AX.X)
                vsync()
                v.op("tensor_scalar", out=MU[:, 2:3], in0=MU[:, 0:1], scalar1=1.0 / D,
                     scalar2=None, op0=ALU.mult)
                vsync()
                v.op("scalar_tensor_tensor", out=JNK[:, :], in0=R1[:, :],
                     scalar=MU[:, 2:3], in1=R1[:, :], op0=ALU.subtract, op1=ALU.mult,
                     accum_out=MU[:, 1:2])
                vsync()
                v.op("tensor_scalar", out=MU[:, 3:4], in0=MU[:, 1:2], scalar1=1.0 / D,
                     scalar2=LNEPS, op0=ALU.mult, op1=ALU.add)
                var_ev = val("v2t")
                s.wait("v2t", var_ev)
                s.op("activation", out=MU[:, 4:5], in_=MU[:, 3:4], func=AF.Ln)
                s.wait("s2v", val("s2v"))
                s.op("activation", out=MU[:, 5:6], in_=MU[:, 4:5], func=AF.Exp,
                     scale=-0.5)
                sq_ev = val("s2v")
                v.wait("s2v", sq_ev)
                v.op("tensor_scalar", out=r1n[:, rsub, :], in0=R1[:, :],
                     scalar1=MU[:, 2:3], scalar2=MU[:, 5:6],
                     op0=ALU.subtract, op1=ALU.mult)

        if debug:
            for rsub in range(16):
                sub = half * 16 + rsub
                v.wait("st", val("st"))
                vsync()
                v.op("tensor_copy", R1[:, :], r1n[:, rsub, :])
                vsync()
                sy.wait("v2t", val("v2t"))
                dma(sy, "st", out=dbg["r1n"][sub * 128:(sub + 1) * 128, :],
                    in_=R1[:, :])
            v.wait("st", val("st"))

        phase_barrier()
        ldw_w1q = []
        ldw_w2q = []
        for q in range(4):
            dma(g, "ldw", out=W1V[:, :, q * 1024:(q + 1) * 1024],
                in_=W1_h.rearrange("(c p) e -> p c e", p=128)[:, :, q * 1024:(q + 1) * 1024])
            ldw_w1q.append(val("ldw"))
            dma(g, "ldw", out=W2V[:, q * 8:(q + 1) * 8, :],
                in_=W2_h.rearrange("(c p) e -> p c e", p=128)[:, q * 8:(q + 1) * 8, :])
            ldw_w2q.append(val("ldw"))
        ldw_p3 = val("ldw")
        phase_reset()

        for tile in range(half * 4, half * 4 + 4):
            for pr in range(2):
                t.wait("v2t", st_["v_free"])
                t.wait("t2s", st_["r1nt_free"])
                for s2 in range(2):
                    sub = tile * 4 + pr * 2 + s2
                    rsub = sub % 16
                    t.wait("v2t", st_["pbT_free"][s2])
                    for c in range(8):
                        t.op("transpose", pbTs[s2][:, c * 128:(c + 1) * 128],
                             r1n[:, rsub, c * 128:(c + 1) * 128], IDN[:, :],
                             inc=[("t2v", 1)])
                    ev = val("t2v")
                    v.wait("t2v", ev)
                    v.wait("t2s", st_["r1nt_free"])
                    v.op("tensor_copy", R1NT[:, :, s2 * 128:(s2 + 1) * 128],
                         pbTs[s2].rearrange("p (c t) -> p c t", c=8))
                    st_["pbT_free"][s2] = val("v2t")
                r1nt_ready = val("v2t")
                t.wait("v2t", r1nt_ready)

                def emit_u(ic):
                    ub = ic % 2
                    t.wait("ldw", ldw_w1q[ic // 8])
                    t.wait("s2v", st_["u_free"][ub])
                    for d in range(8):
                        t.op("matmul", pb[ub][:, 0:256],
                             W1V[:, d, ic * 128:(ic + 1) * 128], R1NT[:, d, :],
                             start=(d == 0), stop=(d == 7),
                             inc=[("t2s", 1)] if d == 7 else ())
                    uev = val("t2s")
                    if ic == 31:
                        st_["r1nt_free"] = uev
                    s.wait("t2s", uev)
                    s.wait("t2s", st_["usb_free"][ub])
                    s.op("activation", out=USB[:, ub, :], in_=pb[ub][:, 0:256],
                         func=AF.Relu, bias=B1COL[:, ic:ic + 1])
                    st_["u_free"][ub] = val("s2v")

                def emit_v(ic):
                    ub = ic % 2
                    t.wait("ldw", ldw_w2q[ic // 8])
                    t.wait("s2v", st_["u_free"][ub])
                    for s2 in range(2):
                        for gr in range(2):
                            vb = VB[s2 * 2 + gr]
                            t.op("matmul", vb[:, :], USB[:, ub, s2 * 128:(s2 + 1) * 128],
                                 W2V[:, ic, gr * 512:(gr + 1) * 512],
                                 start=(ic == 0), stop=False,
                                 inc=[("t2s", 1)] if (s2 == 1 and gr == 1) else ())
                    st_["usb_free"][ub] = val("t2s")

                emit_u(0)
                for ic in range(1, 32):
                    emit_u(ic)
                    emit_v(ic - 1)
                emit_v(31)
                for s2 in range(2):
                    for gr in range(2):
                        vb = VB[s2 * 2 + gr]
                        t.op("matmul", vb[:, :], ONESR[:, :],
                             B2ROW[:, gr * 512:(gr + 1) * 512],
                             start=False, stop=True, inc=[("t2v", 1)])
                vev = val("t2v")
                v.wait("t2v", vev)
                for s2 in range(2):
                    sub = tile * 4 + pr * 2 + s2
                    rsub = sub % 16
                    ob = sub % 2
                    v.wait("st", st_["outs_free"].get(sub - 2, 0))
                    vsync()
                    for gr in range(2):
                        v.op("scalar_tensor_tensor", out=R1[:, gr * 512:(gr + 1) * 512],
                             in0=VB[s2 * 2 + gr][:, :], scalar=1.0,
                             in1=r1n[:, rsub, gr * 512:(gr + 1) * 512],
                             op0=ALU.mult, op1=ALU.add)
                    if s2 == 1:
                        st_["v_free"] = val("v2t")
                    vsync()
                    v.op("reduce_sum", MU[:, 0:1], R1[:, :], axis=AX.X)
                    vsync()
                    v.op("tensor_scalar", out=MU[:, 2:3], in0=MU[:, 0:1],
                         scalar1=1.0 / D, scalar2=None, op0=ALU.mult)
                    vsync()
                    v.op("scalar_tensor_tensor", out=OUTS[:, ob, :], in0=R1[:, :],
                         scalar=MU[:, 2:3], in1=R1[:, :],
                         op0=ALU.subtract, op1=ALU.mult, accum_out=MU[:, 1:2])
                    vsync()
                    v.op("tensor_scalar", out=MU[:, 3:4], in0=MU[:, 1:2],
                         scalar1=1.0 / D, scalar2=LNEPS, op0=ALU.mult, op1=ALU.add)
                    var_ev = val("v2t")
                    s.wait("v2t", var_ev)
                    s.op("activation", out=MU[:, 4:5], in_=MU[:, 3:4], func=AF.Ln)
                    s.wait("s2v", val("s2v"))
                    s.op("activation", out=MU[:, 5:6], in_=MU[:, 4:5], func=AF.Exp,
                         scale=-0.5)
                    sq_ev = val("s2v")
                    v.wait("s2v", sq_ev)
                    v.op("tensor_scalar", out=OUTS[:, ob, :], in0=R1[:, :],
                         scalar1=MU[:, 2:3], scalar2=MU[:, 5:6],
                         op0=ALU.subtract, op1=ALU.mult)
                    oev = val("v2t")
                    sy.wait("v2t", oev)
                    dma(sy, "st", out=out_h[sub * 128:(sub + 1) * 128, :],
                        in_=OUTS[:, ob, :])
                    st_["outs_free"][sub] = val("st")

    sy.wait("st", val("st"))
    g.wait("ldw", val("ldw"))
    g.wait("ldx", val("ldx"))

    blk_cm = nc.Block()
    blk = blk_cm.__enter__()

    def emitter(ename):
        def _(eng):
            for f in ops[ename]:
                f(eng)
        return _
    blk.tensor(emitter("tensor"))
    blk.vector(emitter("vector"))
    blk.scalar(emitter("scalar"))
    blk.sync(emitter("sync"))
    blk.gpsimd(emitter("gpsimd"))
    blk_cm.__exit__(None, None, None)
    for h in reversed(ctx):
        h.__exit__(None, None, None)
    return nc


def make_inputs(x_b, Wq, Wk, Wv, Wo, bo, proj, W1, b1, W2, b2):
    c = np.ascontiguousarray
    ind = np.zeros((8, 128, H), np.float32)
    for ch in range(8):
        ind[ch, 0:64, 2 * ch] = 1.0
        ind[ch, 64:128, 2 * ch + 1] = 1.0
    pt = np.asarray(proj).transpose(0, 2, 1)
    projP = pt.reshape(8, 128, M)
    return {
        "xT": c(np.asarray(x_b).T),
        "x": c(np.asarray(x_b)),
        "WqT": c(np.asarray(Wq).T), "WkT": c(np.asarray(Wk).T),
        "WvT": c(np.asarray(Wv).T), "WoT": c(np.asarray(Wo).T),
        "W1": c(np.asarray(W1)), "W2": c(np.asarray(W2)),
        "projP": c(projP),
        "ind": ind,
        "ident": np.eye(128, dtype=np.float32),
        "onesrow": np.ones((1, 128), np.float32),
        "onescol": np.ones((128, 1), np.float32),
        "borow": c(np.asarray(bo).reshape(1, D)),
        "b2row": c(np.asarray(b2).reshape(1, D)),
        "b1col": c(np.asarray(b1).reshape(32, 128).T),
    }



from concourse.bass_utils import run_bass_kernel_spmd


def kernel(L0_attn_x, L0_attn_Wq, L0_attn_Wk, L0_attn_Wv, L0_attn_Wo, L0_attn_bo,
           L0_attn_proj, L0_ffn_W1, L0_ffn_b1, L0_ffn_W2, L0_ffn_b2):
    x = np.asarray(L0_attn_x)
    nc = build_program(debug=False)
    in_maps = []
    for b in range(8):
        in_maps.append(make_inputs(x[b], L0_attn_Wq, L0_attn_Wk, L0_attn_Wv,
                                   L0_attn_Wo, L0_attn_bo, L0_attn_proj,
                                   L0_ffn_W1, L0_ffn_b1, L0_ffn_W2, L0_ffn_b2))
    res = run_bass_kernel_spmd(nc, in_maps, core_ids=list(range(8)))
    out = np.stack([np.asarray(res.results[b]["out"]) for b in range(8)], axis=0)
    return out.astype(np.float32)


# version 09
# speedup vs baseline: 1.0609x; 1.0609x over previous
import numpy as np
import concourse.bass as bass
import concourse.mybir as mybir

F32 = mybir.dt.float32
BF16 = mybir.dt.bfloat16
AF = mybir.ActivationFunctionType
ALU = mybir.AluOpType
AX = mybir.AxisListType

T, D, H, DK, M, INTER = 4096, 1024, 16, 64, 256, 4096
NT, TS, NSUB = 8, 512, 4
EPS = 1e-6
EPS16 = EPS / 16.0
LN16 = float(np.log(16.0))
LNEPS = 1e-5
ENGS = ("tensor", "vector", "scalar", "sync", "gpsimd")


def build_program(debug=False):
    nc = bass.Bass("TRN2", target_bir_lowering=False)
    din = lambda n, s, dt=F32: nc.dram_tensor(n, s, dt, kind="ExternalInput").ap()
    dout = lambda n, s, dt=F32: nc.dram_tensor(n, s, dt, kind="ExternalOutput").ap()

    xT = din("xT", [D, T])
    x_tm = din("x", [T, D])
    WqT_h = din("WqT", [D, D])
    WkT_h = din("WkT", [D, D])
    WvT_h = din("WvT", [D, D])
    WoT_h = din("WoT", [D, D])
    W1_h = din("W1", [D, INTER])
    W2_h = din("W2", [INTER, D])
    projBD_h = din("projBD", [8, 128, 512])
    ind_h = din("ind", [8, 128, H])
    ident_h = din("ident", [128, 128])
    onesrow_h = din("onesrow", [1, 128])
    onescol_h = din("onescol", [128, 1])
    borow_h = din("borow", [1, D])
    b2row_h = din("b2row", [1, D])
    b1col_h = din("b1col", [128, 32])
    out_h = dout("out", [T, D])
    dbg = {}
    if debug:
        dbg["kv"] = dout("dbg_kv", [128, 2048])
        dbg["ksum"] = dout("dbg_ksum", [128, 32])
        dbg["phik"] = dout("dbg_phik", [128, H * M])
        dbg["r1n"] = dout("dbg_r1n", [T, D])

    ctx = []

    def sb(name, shape, dt):
        h = nc.sbuf_tensor(name, shape, dt)
        t_ = h.__enter__()
        ctx.append(h)
        return t_

    def psum(name, shape, dt):
        h = nc.psum_tensor(name, shape, dt)
        t_ = h.__enter__()
        ctx.append(h)
        return t_

    mega = sb("mega", [128, 64 * 1024], BF16)
    W1V = mega[:, 0:32768].rearrange("p (c e) -> p c e", c=8)
    W2V = mega[:, 32768:65536].rearrange("p (c e) -> p c e", c=32)
    WA = mega[:, 0:8192].rearrange("p (c e) -> p c e", c=8)
    WB = mega[:, 8192:16384].rearrange("p (c e) -> p c e", c=8)
    XB = mega[:, 16384:24576].rearrange("p (b c t) -> p b c t", b=2, c=8)
    KT = mega[:, 24576:28672].rearrange("p (c t) -> p c t", c=8)
    KSQ = mega[:, 28672:32768].rearrange("p (c t) -> p c t", c=8)
    APR = mega[:, 32768:36864].rearrange("p (h m) -> p h m", h=H)
    PHI0 = mega[:, 36864:40960].rearrange("p (h m) -> p h m", h=H)
    PHI1 = mega[:, 47648:51744].rearrange("p (h m) -> p h m", h=H)
    PHIB = [PHI0, PHI1]
    VAUG = mega[:, 40960:42016].rearrange("p (h j) -> p h j", h=H)
    MRG = mega[:, 42016:43040]
    PQT = mega[:, 43040:43552].rearrange("p (b m) -> p b m", b=2)
    PQT8 = mega[:, 45600:47648]
    MGT = mega[:, 43552:44576].rearrange("p (c t) -> p c t", c=8)
    JNK = mega[:, 44576:45600]

    fpool = sb("fpool", [128, 6568], F32)
    KVACC = fpool[:, 0:2080]
    NSQ = fpool[:, 2112:2176].rearrange("p (s h) -> p s h", s=4)
    BIA = fpool[:, 2176:2240].rearrange("p (s h) -> p s h", s=4)
    ENSQ = fpool[:, 2240:2304].rearrange("p (s h) -> p s h", s=4)
    MAXA = fpool[:, 2304:2368].rearrange("p (s h) -> p s h", s=4)
    SFT = fpool[:, 2368:2432].rearrange("p (s h) -> p s h", s=4)
    DEN = fpool[:, 2432:2448]
    RDEN = fpool[:, 2448:2464]
    R1 = fpool[:, 2464:3488]
    XTM = fpool[:, 3488:4512]
    MU = fpool[:, 4512:4520]
    OUTS = fpool[:, 4520:6568].rearrange("p (b e) -> p b e", b=2)

    r1n = sb("r1n", [128, 16, 1024], BF16)
    KVR = sb("kvr", [128, 16, 2, 66], BF16)
    R1NT = sb("r1nt", [128, 8, 256], BF16)
    USB = sb("usb", [128, 2, 256], BF16)
    IDN = sb("idn", [128, 128], BF16)
    IND = sb("indb", [128, 8, 16], BF16)
    PRJBD = sb("prjbd", [128, 8, 512], BF16)
    ONESR = sb("onesr", [1, 128], BF16)
    ONESC = sb("onesc", [128, 1], BF16)
    BOROW = sb("borow_sb", [1, 1024], BF16)
    B2ROW = sb("b2row_sb", [1, 1024], BF16)
    B1COL = sb("b1col_sb", [128, 32], F32)

    pb = [psum(f"pb{i}", [128, 512], F32) for i in range(6)]
    pbT = psum("pbT", [128, 1024], BF16)
    pb6_cm = nc.psum_tensor("pb6", [128, 512], F32)
    pb6 = pb6_cm.__enter__()
    KVB = [pb[3], pb[4], pb[5], pb6]
    NUMB = [pb[2], pb[3]]
    YB = [pb[4], pb[5]]
    VB = [pb[2], pb[3], pb[4], pb[5]]

    sems = {}
    cnt = {}
    for s_ in ("ldw", "ldx", "ldy", "st", "t2v", "t2s", "v2t", "s2v"):
        h = nc.semaphore(s_)
        sems[s_] = h.__enter__()
        ctx.append(h)
        cnt[s_] = 0
    ops = {e: [] for e in ENGS}

    class Rec:
        def __init__(self, eng, auto=None):
            self.e = eng
            self.auto = auto

        def wait(self, sem, v):
            if v > 0:
                ops[self.e].append(lambda g, s=sems[sem], v=v: g.wait_ge(s, v))

        def op(self, meth, *a, inc=(), **kw):
            inc = list(inc)
            if self.auto is not None and meth != "dma_start" and not inc:
                inc = [(self.auto, 1)]
            assert len(inc) <= 1, inc
            incs = []
            for sname, n in inc:
                cnt[sname] += n
                incs.append((sems[sname], n))

            def f(g, meth=meth, a=a, kw=kw, incs=incs):
                r = getattr(g, meth)(*a, **kw)
                for s_, n_ in incs:
                    r = r.then_inc(s_, n_)
            ops[self.e].append(f)

    t = Rec("tensor")
    v = Rec("vector", auto="v2t")
    s = Rec("scalar", auto="s2v")
    sy = Rec("sync")
    g = Rec("gpsimd")
    val = lambda sem: cnt[sem]

    def dma(rec, sem, **kw):
        rec.wait(sem, val(sem))
        rec.op("dma_start", inc=[(sem, 16)], **kw)

    def vsync():
        v.wait("v2t", val("v2t"))

    dma(g, "ldw", out=IDN[:, :], in_=ident_h[:, :])
    dma(g, "ldw", out=IND[:, :, :], in_=ind_h.rearrange("c p h -> p c h"))
    dma(g, "ldw", out=PRJBD[:, :, :], in_=projBD_h.rearrange("c p m -> p c m"))
    dma(g, "ldw", out=ONESR[:, :], in_=onesrow_h[:, :])
    dma(g, "ldw", out=ONESC[:, :], in_=onescol_h[:, :])
    dma(g, "ldw", out=BOROW[:, :], in_=borow_h[:, :])
    dma(g, "ldw", out=B2ROW[:, :], in_=b2row_h[:, :])
    dma(g, "ldw", out=B1COL[:, :], in_=b1col_h[:, :])
    dma(g, "ldw", out=WA[:, :, :], in_=WkT_h.rearrange("(c p) e -> p c e", p=128))
    dma(g, "ldw", out=WB[:, :, :], in_=WvT_h.rearrange("(c p) e -> p c e", p=128))
    ldw_p1 = val("ldw")

    v.op("memset", KVACC[:, :], 0.0)
    v.wait("ldw", 16 * 5)
    v.op("memset", VAUG[:, :, 65:66], 0.0)
    v.op("memset", VAUG[:, :, 64:65], 1.0)
    vaug_ones = val("v2t")
    kvacc_zeroed = val("v2t")

    st_ = dict(
        kproj_free=[0, 0],
        xw_free=[0, 0],
        nsq_bank_free=0,
        kv_banks_free=0,
        kvacc_done=kvacc_zeroed,
        vaug_ready=0,
        phi_free=0,
        phi_free2=[0, 0],
        bia_ready=0,
        ensq_ready=0,
        nsq_evacd=0,
        apr_last=0,
        num_free=0,
        pqt_free=0,
        pqt_free2w=[0, 0],
        pbT_free=[0, 0],
        mrg_free=0,
        mgt_free=0,
        y_free=0,
        kvr_ready=0,
        xtm_free={},
        u_free=[0, 0],
        usb_free=[0, 0],
        r1nt_free=0,
        v_free=0,
        outs_free={},
        xb_ev={},
        xb_rel={},
    )

    def load_xb(tile, phase):
        buf = tile % 2
        key = None
        if phase == "p1" and tile >= 2:
            key = ("p1", tile - 2)
        elif phase == "p2":
            key = ("p1", 6 + tile) if tile < 2 else ("p2", tile - 2)
        if key is not None:
            sem, vv = st_["xb_rel"][key]
            g.wait(sem, vv)
        dma(g, "ldx", out=XB[:, buf, :, :],
            in_=xT.rearrange("(c p) t -> p c t", p=128)[:, :, tile * TS:(tile + 1) * TS])
        st_["xb_ev"][(phase, tile)] = val("ldx")

    def proj_nsq(tile, phase, ldw_cur):
        buf = tile % 2
        t.wait("ldx", st_["xb_ev"][(phase, tile)])
        t.wait("ldw", ldw_cur)
        for c in range(8):
            bank = c % 2
            t.wait("v2t", st_["kproj_free"][bank])
            t.wait("s2v", st_["xw_free"][bank])
            for d in range(8):
                t.op("matmul", pb[bank][:, :], WA[:, d, c * 128:(c + 1) * 128],
                     XB[:, buf, d, :], start=(d == 0), stop=(d == 7),
                     inc=[("t2v", 1)] if d == 7 else ())
            ev = val("t2v")
            if phase == "p2" and c == 7:
                st_["xb_rel"][(phase, tile)] = ("t2v", ev)
            v.wait("t2v", ev)
            v.op("tensor_copy", KT[:, c, :], pb[bank][:, :])
            st_["kproj_free"][bank] = val("v2t")
            vsync()
            v.op("tensor_mul", KSQ[:, c, :], KT[:, c, :], KT[:, c, :])
        ksq_ready = val("v2t")
        t.wait("v2t", ksq_ready)
        t.wait("v2t", st_["nsq_bank_free"])
        t.wait("v2t", st_["num_free"])
        first = True
        for su in range(4):
            for c in range(8):
                last = (su == 3 and c == 7)
                t.op("matmul", pb[2][:, 32 + su * 16:48 + su * 16],
                     KSQ[:, c, su * 128:(su + 1) * 128], IND[:, c, :],
                     start=first, stop=last,
                     inc=[("t2v", 1)] if last else ())
                first = False
        nsq_mm = val("t2v")
        v.wait("t2v", nsq_mm)
        v.op("tensor_copy", NSQ[:, :, :],
             pb[2].rearrange("p (a b) -> p a b", b=16)[:, 2:6, :])
        st_["nsq_evacd"] = val("v2t")
        v.wait("s2v", max(st_["ensq_ready"], st_["apr_last"]))
        vsync()
        v.op("tensor_scalar", out=BIA[:, :, :].rearrange("p s h -> p (s h)"),
             in0=NSQ[:, :, :].rearrange("p s h -> p (s h)"), scalar1=-0.5 / 64.0,
             scalar2=-LN16, op0=ALU.mult, op1=ALU.add)
        st_["bia_ready"] = val("v2t")
        s.wait("v2t", st_["bia_ready"])
        s.op("activation", out=ENSQ[:, :, :].rearrange("p s h -> p (s h)"),
             in_=BIA[:, :, :].rearrange("p s h -> p (s h)"), func=AF.Exp)
        st_["ensq_ready"] = val("s2v")

    def emit_xw_pair(su, p):
        bank = pb[p % 2]
        t.wait("s2v", st_["xw_free"][p % 2])
        t.wait("v2t", max(st_["kproj_free"]))
        t.op("matmul", bank[:, 0:512], KT[:, p, su * 128:(su + 1) * 128],
             PRJBD[:, p, :], start=True, stop=True, inc=[("t2s", 1)])
        xw_ev = val("t2s")
        s.wait("t2s", xw_ev)
        s.wait("v2t", max(st_["bia_ready"], st_.get("apr_free_v2t", 0)))
        for h2 in range(2):
            h = 2 * p + h2
            s.op("activation", out=APR[:, h, :],
                 in_=bank[:, h2 * 256:(h2 + 1) * 256],
                 func=AF.Exp, bias=BIA[:, su, h:h + 1], scale=0.125)
        st_["xw_free"][p % 2] = val("s2v")
        if p == 7:
            st_["apr_last"] = val("s2v")

        PHI = PHIB[su % 2]
        pst = st_.setdefault("phipipe", {})

        def stage_reduce(q):
            sl = slice(2 * q, 2 * q + 2)
            v.wait("s2v", pst[("exps", su, q)])
            if q == 0:
                v.wait("s2v", st_["ensq_ready"])
                v.wait("t2v", st_["phi_free2"][su % 2])
            v.op("reduce_max", MAXA[:, su, sl], APR[:, sl, :], axis=AX.X)
            pst[("red", su, q)] = val("v2t")

        def stage_rest(q):
            sl = slice(2 * q, 2 * q + 2)
            v.wait("v2t", pst[("red", su, q)])
            v.op("reciprocal", SFT[:, su, sl], MAXA[:, su, sl])
            rc = val("v2t")
            v.wait("v2t", rc)
            v.op("tensor_mul", SFT[:, su, sl], SFT[:, su, sl], ENSQ[:, su, sl])
            mv = val("v2t")
            v.wait("v2t", mv)
            for h in (2 * q, 2 * q + 1):
                v.op("tensor_scalar", out=PHI[:, h, :], in0=APR[:, h, :],
                     scalar1=SFT[:, su, h:h + 1], scalar2=EPS16,
                     op0=ALU.mult, op1=ALU.add)

        pst[("exps", su, p)] = val("s2v")
        stage_reduce(p)
        if p > 0:
            stage_rest(p - 1)
        if p == 7:
            stage_rest(7)
            st_["apr_free_v2t"] = val("v2t")
        return val("v2t")

    def emit_phi_vec(su):
        return val("v2t")

    def emit_kv(su, h0, h1, phi_ready_su):
        firstk = (su == 0)
        t.wait("v2t", phi_ready_su)
        if h0 == 0:
            t.wait("v2t", st_["vaug_ready_kv"])
            if su == 0:
                t.wait("v2t", st_["kv_banks_free"])
                t.wait("v2t", st_["nsq_evacd"])
                t.wait("v2t", st_["nsq_bank_free"])
        PHI = PHIB[su % 2]
        for h in range(h0, h1):
            for c in range(2):
                idx = h * 2 + c
                if idx < 28:
                    bank, slot = KVB[idx // 7], idx % 7
                else:
                    bank, slot = pb[2], idx - 28
                stop_kv = (su == 3 and (slot == 6 or idx == 31))
                t.op("matmul", bank[:, slot * 65:slot * 65 + 65],
                     PHI[:, h, c * 128:(c + 1) * 128], VAUG[:, h, 0:65],
                     start=(firstk and slot == 0), stop=stop_kv,
                     inc=[("t2v", 1)] if (stop_kv or idx == 31) else ())
        if h1 == H:
            st_["phi_free2"][su % 2] = val("t2v")

    st_["vaug_ready_kv"] = 0
    for tile in range(NT):
        load_xb(tile, "p1")
        proj_nsq(tile, "p1", ldw_p1)
        prev_phi_ready = st_.get("p1_phi_ready", 0)
        for su in range(4):
            buf = tile % 2
            for p in range(8):
                ph = emit_xw_pair(su, p)
                if su > 0 and p < 4:
                    emit_kv(su - 1, p * 4, (p + 1) * 4, prev_phi_ready)
            cur_phi_ready = ph
            for gr in range(2):
                bank = gr
                t.wait("s2v", st_["xw_free"][bank])
                t.wait("v2t", st_["kproj_free"][bank])
                for d in range(8):
                    lastd = (d == 7)
                    t.op("matmul", pb[bank][:, :],
                         XB[:, buf, d, su * 128:(su + 1) * 128],
                         WB[:, d, gr * 512:(gr + 1) * 512],
                         start=(d == 0), stop=lastd,
                         inc=[("t2v", 1)] if lastd else ())
                ev = val("t2v")
                if su == 3 and gr == 1:
                    st_["xb_rel"][("p1", tile)] = ("t2v", ev)
                v.wait("t2v", ev)
                v.wait("t2v", st_["phi_free2"][(su - 1) % 2])
                if tile == 0 and su == 0 and gr == 0:
                    v.wait("v2t", vaug_ones)
                v.op("tensor_copy", VAUG[:, gr * 8:(gr + 1) * 8, 0:64],
                     pb[bank].rearrange("p (h e) -> p h e", h=8)[:, :, :])
                st_["kproj_free"][bank] = val("v2t")
            st_["vaug_ready_kv"] = val("v2t")
            prev_phi_ready = cur_phi_ready
            st_["p1_phi_ready"] = prev_phi_ready
        emit_kv(3, 0, H, prev_phi_ready)
        kv_done = val("t2v")
        v.wait("t2v", kv_done)
        v.wait("v2t", st_["kvacc_done"])
        for b in range(4):
            v.op("tensor_add", KVACC[:, b * 455:(b + 1) * 455],
                 KVACC[:, b * 455:(b + 1) * 455], KVB[b][:, 0:455])
        v.op("tensor_add", KVACC[:, 1820:2080], KVACC[:, 1820:2080], pb[2][:, 0:260])
        st_["kvacc_done"] = val("v2t")
        st_["kv_banks_free"] = val("v2t")
        st_["nsq_bank_free"] = val("v2t")

    vsync()
    v.op("tensor_copy", KVR[:, :, :, 0:65],
         KVACC.rearrange("p (h c e) -> p h c e", h=16, c=2))
    v.op("memset", KVR[:, :, :, 65:66], 0.0)
    st_["kvr_ready"] = val("v2t")
    if debug:
        vsync()
        v.op("tensor_copy", R1.rearrange("p (i j) -> p i j", j=64),
             KVACC.rearrange("p (i j) -> p i j", j=65)[:, 0:16, 0:64])
        vsync()
        sy.wait("v2t", val("v2t"))
        dma(sy, "st", out=dbg["kv"][:, 0:1024], in_=R1[:, :])
        v.wait("st", val("st"))
        v.op("tensor_copy", R1.rearrange("p (i j) -> p i j", j=64),
             KVACC.rearrange("p (i j) -> p i j", j=65)[:, 16:32, 0:64])
        vsync()
        sy.wait("v2t", val("v2t"))
        dma(sy, "st", out=dbg["kv"][:, 1024:2048], in_=R1[:, :])
        v.wait("st", val("st"))
        v.op("tensor_copy", R1[:, 0:32].rearrange("p (i j) -> p i j", j=1),
             KVACC.rearrange("p (i j) -> p i j", j=65)[:, :, 64:65])
        vsync()
        sy.wait("v2t", val("v2t"))
        dma(sy, "st", out=dbg["ksum"][:, :], in_=R1[:, 0:32])
        v.wait("st", val("st"))

    pb6_cm.__exit__(None, None, None)
    pbT2 = nc.psum_tensor("pbT2", [128, 1024], BF16)
    pbT2t = pbT2.__enter__()
    ctx.append(pbT2)
    pbTs = [pbT, pbT2t]

    def phase_reset():
        st_["kproj_free"] = [val("v2t"), val("v2t")]
        st_["xw_free"] = [val("s2v"), val("s2v")]
        st_["nsq_bank_free"] = val("v2t")
        st_["num_free"] = val("v2t")
        st_["y_free"] = val("v2t")
        st_["v_free"] = val("v2t")
        st_["mgt_free"] = val("t2v")
        st_["mrg_free"] = val("t2v")
        st_["pbT_free"] = [val("v2t"), val("v2t")]
        st_["pqt_free"] = val("t2v")
        st_["pqt_free2w"] = [val("t2v"), val("t2v")]
        st_["phi_free"] = val("t2v")
        st_["phi_free2"] = [val("t2v"), val("t2v")]
        st_["u_free"] = [val("s2v"), val("s2v")]
        st_["usb_free"] = [val("t2s"), val("t2s")]
        st_["r1nt_free"] = val("t2s")

    def phase_barrier():
        g.wait("t2v", val("t2v"))
        g.wait("t2s", val("t2s"))
        g.wait("v2t", val("v2t"))
        g.wait("s2v", val("s2v"))

    for half in range(2):
        phase_barrier()
        dma(g, "ldw", out=WA[:, :, :], in_=WqT_h.rearrange("(c p) e -> p c e", p=128))
        dma(g, "ldw", out=WB[:, :, :], in_=WoT_h.rearrange("(c p) e -> p c e", p=128))
        ldw_p2 = val("ldw")
        phase_reset()

        for tile in range(half * 4, half * 4 + 4):
            load_xb(tile, "p2")
            proj_nsq(tile, "p2", ldw_p2)
            pend = []
            for p in range(8):
                st_["p2_phi_ready"] = emit_xw_pair(0, p)
            for su in range(4):
                sub = tile * 4 + su
                rsub = sub % 16
                if sub >= 1:
                    sy.wait("v2t", st_["xtm_free"].get(sub - 1, 0))
                dma(sy, "ldy", out=XTM[:, :],
                    in_=x_tm[sub * 128:(sub + 1) * 128, :])
                xtm_ev = val("ldy")
                phi_ready = st_["p2_phi_ready"]
                pend = list(range(8)) if su < 3 else []

                def drip(k):
                    for _ in range(k):
                        if pend:
                            st_["p2_phi_ready"] = emit_xw_pair(su + 1, pend.pop(0))

                PHI = PHIB[su % 2]
                t.wait("v2t", st_["kvr_ready"])
                wave_trev = {}
                wave_evac = {}
                grp_start_done = [False, False]

                def emit_T(w):
                    bnk = w % 2
                    t.wait("v2t", phi_ready)
                    t.wait("v2t", st_["pbT_free"][bnk])
                    t.wait("v2t", st_["num_free"] if w < 2 else 0)
                    for hh4 in range(4):
                        h = w * 4 + hh4
                        for c in range(2):
                            slot = hh4 * 2 + c
                            t.op("transpose", pbTs[bnk][:, slot * 128:(slot + 1) * 128],
                                 PHI[:, h, c * 128:(c + 1) * 128], IDN[:, :],
                                 inc=[("t2v", 1)])
                    wave_trev[w] = val("t2v")
                    if w == 3:
                        st_["phi_free2"][su % 2] = wave_trev[w]

                def emit_evac(w):
                    bnk = w % 2
                    v.wait("t2v", wave_trev[w])
                    v.wait("t2v", st_["pqt_free2w"][w % 2])
                    v.op("tensor_copy", PQT8[:, (w % 2) * 1024:(w % 2) * 1024 + 1024],
                         pbTs[bnk][:, :])
                    st_["pbT_free"][bnk] = val("v2t")
                    wave_evac[w] = val("v2t")

                def emit_num(w):
                    grp = w // 2
                    t.wait("v2t", wave_evac[w])
                    if w % 2 == 0:
                        t.wait("v2t", st_["num_free"])
                    for hh4 in range(4):
                        hh = (w % 2) * 4 + hh4
                        h = w * 4 + hh4
                        nb = NUMB[w % 2]
                        nslot = hh4
                        base = (w % 2) * 1024 + hh4 * 256
                        for c in range(2):
                            t.op("matmul", nb[:, nslot * 128:nslot * 128 + 66],
                                 PQT8[:, base + c * 128:base + c * 128 + 128],
                                 KVR[:, h, c, :],
                                 start=(c == 0 and hh4 == 0),
                                 stop=(nslot == 3 and c == 1),
                                 inc=[("t2v", 1)] if c == 1 else ())
                    st_["pqt_free2w"][w % 2] = val("t2v")

                emit_T(0)
                emit_evac(0)
                for w in range(4):
                    if w < 3:
                        emit_T(w + 1)
                        emit_evac(w + 1)
                    emit_num(w)
                    drip(1)
                    if w % 2 == 1:
                        numg = val("t2v")
                        v.wait("t2v", numg)
                        base_h = (w - 1) * 4
                        for bb in range(2):
                            v.op("tensor_scalar",
                                 out=DEN[:, base_h + bb * 4:base_h + bb * 4 + 4],
                                 in0=NUMB[bb].rearrange("p (a b) -> p a b", b=128)[:, :, 64:65]
                                 .rearrange("p a b -> p (a b)"),
                                 scalar1=EPS, scalar2=None, op0=ALU.add)
                        vsync()
                        v.op("reciprocal", RDEN[:, base_h:base_h + 8],
                             DEN[:, base_h:base_h + 8])
                        vsync()
                        v.wait("t2v", st_["mrg_free"])
                        for hh in range(8):
                            h = base_h + hh
                            v.op("tensor_scalar", out=MRG[:, h * 64:(h + 1) * 64],
                                 in0=NUMB[hh // 4][:, (hh % 4) * 128:(hh % 4) * 128 + 64],
                                 scalar1=RDEN[:, h:h + 1], scalar2=None, op0=ALU.mult)
                        st_["num_free"] = val("v2t")
                        drip(1)
                mrg_ready = val("v2t")
                t.wait("v2t", mrg_ready)
                t.wait("v2t", st_["pbT_free"][0])
                for c in range(8):
                    t.op("transpose", pbTs[0][:, c * 128:(c + 1) * 128],
                         MRG[:, c * 128:(c + 1) * 128], IDN[:, :],
                         inc=[("t2v", 1)])
                st_["mrg_free"] = val("t2v")
                drip(2)
                v.wait("t2v", st_["mrg_free"])
                v.wait("t2v", st_["mgt_free"])
                v.op("tensor_copy", MGT[:, :, :], pbTs[0].rearrange("p (c t) -> p c t", c=8))
                st_["pbT_free"][0] = val("v2t")
                mgt_ready = val("v2t")
                t.wait("v2t", mgt_ready)
                t.wait("v2t", st_["y_free"])
                for gr in range(2):
                    for ic in range(8):
                        t.op("matmul", YB[gr][:, :], MGT[:, ic, :],
                             WB[:, ic, gr * 512:(gr + 1) * 512],
                             start=(ic == 0), stop=False)
                    t.op("matmul", YB[gr][:, :], ONESR[:, :],
                         BOROW[:, gr * 512:(gr + 1) * 512],
                         start=False, stop=True, inc=[("t2v", 1)])
                st_["mgt_free"] = val("t2v")
                yev = val("t2v")
                drip(8)
                v.wait("t2v", yev)
                v.wait("ldy", xtm_ev)
                vsync()
                for gr in range(2):
                    v.op("scalar_tensor_tensor", out=R1[:, gr * 512:(gr + 1) * 512],
                         in0=YB[gr][:, :], scalar=1.0,
                         in1=XTM[:, gr * 512:(gr + 1) * 512],
                         op0=ALU.mult, op1=ALU.add)
                st_["y_free"] = val("v2t")
                st_["xtm_free"][sub] = val("v2t")
                vsync()
                v.op("reduce_sum", MU[:, 0:1], R1[:, :], axis=AX.X)
                vsync()
                v.op("tensor_scalar", out=MU[:, 2:3], in0=MU[:, 0:1], scalar1=1.0 / D,
                     scalar2=None, op0=ALU.mult)
                vsync()
                v.op("scalar_tensor_tensor", out=JNK[:, :], in0=R1[:, :],
                     scalar=MU[:, 2:3], in1=R1[:, :], op0=ALU.subtract, op1=ALU.mult,
                     accum_out=MU[:, 1:2])
                vsync()
                v.op("tensor_scalar", out=MU[:, 3:4], in0=MU[:, 1:2], scalar1=1.0 / D,
                     scalar2=LNEPS, op0=ALU.mult, op1=ALU.add)
                var_ev = val("v2t")
                s.wait("v2t", var_ev)
                s.op("activation", out=MU[:, 4:5], in_=MU[:, 3:4], func=AF.Ln)
                s.wait("s2v", val("s2v"))
                s.op("activation", out=MU[:, 5:6], in_=MU[:, 4:5], func=AF.Exp,
                     scale=-0.5)
                sq_ev = val("s2v")
                v.wait("s2v", sq_ev)
                v.op("tensor_scalar", out=r1n[:, rsub, :], in0=R1[:, :],
                     scalar1=MU[:, 2:3], scalar2=MU[:, 5:6],
                     op0=ALU.subtract, op1=ALU.mult)

        if debug:
            for rsub in range(16):
                sub = half * 16 + rsub
                v.wait("st", val("st"))
                vsync()
                v.op("tensor_copy", R1[:, :], r1n[:, rsub, :])
                vsync()
                sy.wait("v2t", val("v2t"))
                dma(sy, "st", out=dbg["r1n"][sub * 128:(sub + 1) * 128, :],
                    in_=R1[:, :])
            v.wait("st", val("st"))

        phase_barrier()
        ldw_w1q = []
        ldw_w2q = []
        for q in range(4):
            dma(g, "ldw", out=W1V[:, :, q * 1024:(q + 1) * 1024],
                in_=W1_h.rearrange("(c p) e -> p c e", p=128)[:, :, q * 1024:(q + 1) * 1024])
            ldw_w1q.append(val("ldw"))
            dma(g, "ldw", out=W2V[:, q * 8:(q + 1) * 8, :],
                in_=W2_h.rearrange("(c p) e -> p c e", p=128)[:, q * 8:(q + 1) * 8, :])
            ldw_w2q.append(val("ldw"))
        ldw_p3 = val("ldw")
        phase_reset()

        for tile in range(half * 4, half * 4 + 4):
            for pr in range(2):
                t.wait("v2t", st_["v_free"])
                t.wait("t2s", st_["r1nt_free"])
                for s2 in range(2):
                    sub = tile * 4 + pr * 2 + s2
                    rsub = sub % 16
                    t.wait("v2t", st_["pbT_free"][s2])
                    for c in range(8):
                        t.op("transpose", pbTs[s2][:, c * 128:(c + 1) * 128],
                             r1n[:, rsub, c * 128:(c + 1) * 128], IDN[:, :],
                             inc=[("t2v", 1)])
                    ev = val("t2v")
                    v.wait("t2v", ev)
                    v.wait("t2s", st_["r1nt_free"])
                    v.op("tensor_copy", R1NT[:, :, s2 * 128:(s2 + 1) * 128],
                         pbTs[s2].rearrange("p (c t) -> p c t", c=8))
                    st_["pbT_free"][s2] = val("v2t")
                r1nt_ready = val("v2t")
                t.wait("v2t", r1nt_ready)

                def emit_u(ic):
                    ub = ic % 2
                    t.wait("ldw", ldw_w1q[ic // 8])
                    t.wait("s2v", st_["u_free"][ub])
                    for d in range(8):
                        t.op("matmul", pb[ub][:, 0:256],
                             W1V[:, d, ic * 128:(ic + 1) * 128], R1NT[:, d, :],
                             start=(d == 0), stop=(d == 7),
                             inc=[("t2s", 1)] if d == 7 else ())
                    uev = val("t2s")
                    if ic == 31:
                        st_["r1nt_free"] = uev
                    s.wait("t2s", uev)
                    s.wait("t2s", st_["usb_free"][ub])
                    s.op("activation", out=USB[:, ub, :], in_=pb[ub][:, 0:256],
                         func=AF.Relu, bias=B1COL[:, ic:ic + 1])
                    st_["u_free"][ub] = val("s2v")

                def emit_v(ic):
                    ub = ic % 2
                    t.wait("ldw", ldw_w2q[ic // 8])
                    t.wait("s2v", st_["u_free"][ub])
                    for s2 in range(2):
                        for gr in range(2):
                            vb = VB[s2 * 2 + gr]
                            t.op("matmul", vb[:, :], USB[:, ub, s2 * 128:(s2 + 1) * 128],
                                 W2V[:, ic, gr * 512:(gr + 1) * 512],
                                 start=(ic == 0), stop=False,
                                 inc=[("t2s", 1)] if (s2 == 1 and gr == 1) else ())
                    st_["usb_free"][ub] = val("t2s")

                emit_u(0)
                for ic in range(1, 32):
                    emit_u(ic)
                    emit_v(ic - 1)
                emit_v(31)
                for s2 in range(2):
                    for gr in range(2):
                        vb = VB[s2 * 2 + gr]
                        t.op("matmul", vb[:, :], ONESR[:, :],
                             B2ROW[:, gr * 512:(gr + 1) * 512],
                             start=False, stop=True, inc=[("t2v", 1)])
                vev = val("t2v")
                v.wait("t2v", vev)
                for s2 in range(2):
                    sub = tile * 4 + pr * 2 + s2
                    rsub = sub % 16
                    ob = sub % 2
                    v.wait("st", st_["outs_free"].get(sub - 2, 0))
                    vsync()
                    for gr in range(2):
                        v.op("scalar_tensor_tensor", out=R1[:, gr * 512:(gr + 1) * 512],
                             in0=VB[s2 * 2 + gr][:, :], scalar=1.0,
                             in1=r1n[:, rsub, gr * 512:(gr + 1) * 512],
                             op0=ALU.mult, op1=ALU.add)
                    if s2 == 1:
                        st_["v_free"] = val("v2t")
                    vsync()
                    v.op("reduce_sum", MU[:, 0:1], R1[:, :], axis=AX.X)
                    vsync()
                    v.op("tensor_scalar", out=MU[:, 2:3], in0=MU[:, 0:1],
                         scalar1=1.0 / D, scalar2=None, op0=ALU.mult)
                    vsync()
                    v.op("scalar_tensor_tensor", out=OUTS[:, ob, :], in0=R1[:, :],
                         scalar=MU[:, 2:3], in1=R1[:, :],
                         op0=ALU.subtract, op1=ALU.mult, accum_out=MU[:, 1:2])
                    vsync()
                    v.op("tensor_scalar", out=MU[:, 3:4], in0=MU[:, 1:2],
                         scalar1=1.0 / D, scalar2=LNEPS, op0=ALU.mult, op1=ALU.add)
                    var_ev = val("v2t")
                    s.wait("v2t", var_ev)
                    s.op("activation", out=MU[:, 4:5], in_=MU[:, 3:4], func=AF.Ln)
                    s.wait("s2v", val("s2v"))
                    s.op("activation", out=MU[:, 5:6], in_=MU[:, 4:5], func=AF.Exp,
                         scale=-0.5)
                    sq_ev = val("s2v")
                    v.wait("s2v", sq_ev)
                    v.op("tensor_scalar", out=OUTS[:, ob, :], in0=R1[:, :],
                         scalar1=MU[:, 2:3], scalar2=MU[:, 5:6],
                         op0=ALU.subtract, op1=ALU.mult)
                    oev = val("v2t")
                    sy.wait("v2t", oev)
                    dma(sy, "st", out=out_h[sub * 128:(sub + 1) * 128, :],
                        in_=OUTS[:, ob, :])
                    st_["outs_free"][sub] = val("st")

    sy.wait("st", val("st"))
    g.wait("ldw", val("ldw"))
    g.wait("ldx", val("ldx"))

    blk_cm = nc.Block()
    blk = blk_cm.__enter__()

    def emitter(ename):
        def _(eng):
            for f in ops[ename]:
                f(eng)
        return _
    blk.tensor(emitter("tensor"))
    blk.vector(emitter("vector"))
    blk.scalar(emitter("scalar"))
    blk.sync(emitter("sync"))
    blk.gpsimd(emitter("gpsimd"))
    blk_cm.__exit__(None, None, None)
    for h in reversed(ctx):
        h.__exit__(None, None, None)
    return nc


def make_inputs(x_b, Wq, Wk, Wv, Wo, bo, proj, W1, b1, W2, b2):
    c = np.ascontiguousarray
    ind = np.zeros((8, 128, H), np.float32)
    for ch in range(8):
        ind[ch, 0:64, 2 * ch] = 1.0
        ind[ch, 64:128, 2 * ch + 1] = 1.0
    pt = np.asarray(proj).transpose(0, 2, 1)
    projBD = np.zeros((8, 128, 512), np.float32)
    projBD[:, 0:64, 0:256] = pt[0::2]
    projBD[:, 64:128, 256:512] = pt[1::2]
    return {
        "xT": c(np.asarray(x_b).T),
        "x": c(np.asarray(x_b)),
        "WqT": c(np.asarray(Wq).T), "WkT": c(np.asarray(Wk).T),
        "WvT": c(np.asarray(Wv).T), "WoT": c(np.asarray(Wo).T),
        "W1": c(np.asarray(W1)), "W2": c(np.asarray(W2)),
        "projBD": c(projBD),
        "ind": ind,
        "ident": np.eye(128, dtype=np.float32),
        "onesrow": np.ones((1, 128), np.float32),
        "onescol": np.ones((128, 1), np.float32),
        "borow": c(np.asarray(bo).reshape(1, D)),
        "b2row": c(np.asarray(b2).reshape(1, D)),
        "b1col": c(np.asarray(b1).reshape(32, 128).T),
    }



from concourse.bass_utils import run_bass_kernel_spmd


def kernel(L0_attn_x, L0_attn_Wq, L0_attn_Wk, L0_attn_Wv, L0_attn_Wo, L0_attn_bo,
           L0_attn_proj, L0_ffn_W1, L0_ffn_b1, L0_ffn_W2, L0_ffn_b2):
    x = np.asarray(L0_attn_x)
    nc = build_program(debug=False)
    in_maps = []
    for b in range(8):
        in_maps.append(make_inputs(x[b], L0_attn_Wq, L0_attn_Wk, L0_attn_Wv,
                                   L0_attn_Wo, L0_attn_bo, L0_attn_proj,
                                   L0_ffn_W1, L0_ffn_b1, L0_ffn_W2, L0_ffn_b2))
    res = run_bass_kernel_spmd(nc, in_maps, core_ids=list(range(8)))
    out = np.stack([np.asarray(res.results[b]["out"]) for b in range(8)], axis=0)
    return out.astype(np.float32)


# version 10
# speedup vs baseline: 1.0794x; 1.0174x over previous
import numpy as np
import concourse.bass as bass
import concourse.mybir as mybir

F32 = mybir.dt.float32
BF16 = mybir.dt.bfloat16
AF = mybir.ActivationFunctionType
ALU = mybir.AluOpType
AX = mybir.AxisListType

T, D, H, DK, M, INTER = 4096, 1024, 16, 64, 256, 4096
NT, TS, NSUB = 8, 512, 4
EPS = 1e-6
EPS16 = EPS / 16.0
LN16 = float(np.log(16.0))
LNEPS = 1e-5
ENGS = ("tensor", "vector", "scalar", "sync", "gpsimd")


def build_program(debug=False):
    nc = bass.Bass("TRN2", target_bir_lowering=False)
    din = lambda n, s, dt=F32: nc.dram_tensor(n, s, dt, kind="ExternalInput").ap()
    dout = lambda n, s, dt=F32: nc.dram_tensor(n, s, dt, kind="ExternalOutput").ap()

    xT = din("xT", [D, T])
    x_tm = din("x", [T, D])
    WqT_h = din("WqT", [D, D])
    WkT_h = din("WkT", [D, D])
    WvT_h = din("WvT", [D, D])
    WoT_h = din("WoT", [D, D])
    W1_h = din("W1", [D, INTER])
    W2_h = din("W2", [INTER, D])
    projBD_h = din("projBD", [8, 128, 512])
    ind_h = din("ind", [8, 128, H])
    ident_h = din("ident", [128, 128])
    onesrow_h = din("onesrow", [1, 128])
    onescol_h = din("onescol", [128, 1])
    borow_h = din("borow", [1, D])
    b2row_h = din("b2row", [1, D])
    b1col_h = din("b1col", [128, 32])
    out_h = dout("out", [T, D])
    dbg = {}
    if debug:
        dbg["kv"] = dout("dbg_kv", [128, 2048])
        dbg["ksum"] = dout("dbg_ksum", [128, 32])
        dbg["phik"] = dout("dbg_phik", [128, H * M])
        dbg["r1n"] = dout("dbg_r1n", [T, D])

    ctx = []

    def sb(name, shape, dt):
        h = nc.sbuf_tensor(name, shape, dt)
        t_ = h.__enter__()
        ctx.append(h)
        return t_

    def psum(name, shape, dt):
        h = nc.psum_tensor(name, shape, dt)
        t_ = h.__enter__()
        ctx.append(h)
        return t_

    mega = sb("mega", [128, 64 * 1024], BF16)
    W1V = mega[:, 0:32768].rearrange("p (c e) -> p c e", c=8)
    W2V = mega[:, 32768:65536].rearrange("p (c e) -> p c e", c=32)
    WA = mega[:, 0:8192].rearrange("p (c e) -> p c e", c=8)
    WB = mega[:, 8192:16384].rearrange("p (c e) -> p c e", c=8)
    XB = mega[:, 16384:24576].rearrange("p (b c t) -> p b c t", b=2, c=8)
    KT = mega[:, 24576:28672].rearrange("p (c t) -> p c t", c=8)
    KSQ = mega[:, 28672:32768].rearrange("p (c t) -> p c t", c=8)
    APR = mega[:, 32768:36864].rearrange("p (h m) -> p h m", h=H)
    PHI0 = mega[:, 36864:40960].rearrange("p (h m) -> p h m", h=H)
    PHI1 = mega[:, 47648:51744].rearrange("p (h m) -> p h m", h=H)
    PHIB = [PHI0, PHI1]
    VAUG = mega[:, 40960:42016].rearrange("p (h j) -> p h j", h=H)
    MRG = mega[:, 42016:43040]
    PQT = mega[:, 43040:43552].rearrange("p (b m) -> p b m", b=2)
    PQT8 = mega[:, 45600:47648]
    MGT = mega[:, 43552:44576].rearrange("p (c t) -> p c t", c=8)
    JNK = mega[:, 44576:45600]

    fpool = sb("fpool", [128, 6568], F32)
    KVACC = fpool[:, 0:2080]
    NSQ = fpool[:, 2112:2176].rearrange("p (s h) -> p s h", s=4)
    BIA = fpool[:, 2176:2240].rearrange("p (s h) -> p s h", s=4)
    ENSQ = fpool[:, 2240:2304].rearrange("p (s h) -> p s h", s=4)
    MAXA = fpool[:, 2304:2368].rearrange("p (s h) -> p s h", s=4)
    SFT = fpool[:, 2368:2432].rearrange("p (s h) -> p s h", s=4)
    DEN = fpool[:, 2432:2448]
    RDEN = fpool[:, 2448:2464]
    R1 = fpool[:, 2464:3488]
    XTM = fpool[:, 3488:4512]
    MU = fpool[:, 4512:4520]
    OUTS = fpool[:, 4520:6568].rearrange("p (b e) -> p b e", b=2)

    r1n = sb("r1n", [128, 16, 1024], BF16)
    KVR = sb("kvr", [128, 16, 2, 66], BF16)
    R1NT = sb("r1nt", [128, 8, 256], BF16)
    USB = sb("usb", [128, 2, 256], BF16)
    IDN = sb("idn", [128, 128], BF16)
    IND = sb("indb", [128, 8, 16], BF16)
    PRJBD = sb("prjbd", [128, 8, 512], BF16)
    ONESR = sb("onesr", [1, 128], BF16)
    ONESC = sb("onesc", [128, 1], BF16)
    BOROW = sb("borow_sb", [1, 1024], BF16)
    B2ROW = sb("b2row_sb", [1, 1024], BF16)
    B1COL = sb("b1col_sb", [128, 32], F32)

    pb = [psum(f"pb{i}", [128, 512], F32) for i in range(6)]
    pbT = psum("pbT", [128, 1024], BF16)
    pb6_cm = nc.psum_tensor("pb6", [128, 512], F32)
    pb6 = pb6_cm.__enter__()
    KVB = [pb[3], pb[4], pb[5], pb6]
    NUMB = [pb[2], pb[3]]
    YB = [pb[4], pb[5]]
    VB = [pb[2], pb[3], pb[4], pb[5]]

    sems = {}
    cnt = {}
    for s_ in ("ldw", "ldx", "ldy", "st", "t2v", "t2s", "v2t", "s2v"):
        h = nc.semaphore(s_)
        sems[s_] = h.__enter__()
        ctx.append(h)
        cnt[s_] = 0
    ops = {e: [] for e in ENGS}

    class Rec:
        def __init__(self, eng, auto=None):
            self.e = eng
            self.auto = auto

        def wait(self, sem, v):
            if v > 0:
                ops[self.e].append(lambda g, s=sems[sem], v=v: g.wait_ge(s, v))

        def op(self, meth, *a, inc=(), **kw):
            inc = list(inc)
            if self.auto is not None and meth != "dma_start" and not inc:
                inc = [(self.auto, 1)]
            assert len(inc) <= 1, inc
            incs = []
            for sname, n in inc:
                cnt[sname] += n
                incs.append((sems[sname], n))

            def f(g, meth=meth, a=a, kw=kw, incs=incs):
                r = getattr(g, meth)(*a, **kw)
                for s_, n_ in incs:
                    r = r.then_inc(s_, n_)
            ops[self.e].append(f)

    t = Rec("tensor")
    v = Rec("vector", auto="v2t")
    s = Rec("scalar", auto="s2v")
    sy = Rec("sync")
    g = Rec("gpsimd")
    val = lambda sem: cnt[sem]

    def dma(rec, sem, **kw):
        rec.wait(sem, val(sem))
        rec.op("dma_start", inc=[(sem, 16)], **kw)

    def vsync():
        v.wait("v2t", val("v2t"))

    dma(g, "ldw", out=IDN[:, :], in_=ident_h[:, :])
    dma(g, "ldw", out=IND[:, :, :], in_=ind_h.rearrange("c p h -> p c h"))
    dma(g, "ldw", out=PRJBD[:, :, :], in_=projBD_h.rearrange("c p m -> p c m"))
    dma(g, "ldw", out=ONESR[:, :], in_=onesrow_h[:, :])
    dma(g, "ldw", out=ONESC[:, :], in_=onescol_h[:, :])
    dma(g, "ldw", out=BOROW[:, :], in_=borow_h[:, :])
    dma(g, "ldw", out=B2ROW[:, :], in_=b2row_h[:, :])
    dma(g, "ldw", out=B1COL[:, :], in_=b1col_h[:, :])
    dma(g, "ldw", out=WA[:, :, :], in_=WkT_h.rearrange("(c p) e -> p c e", p=128))
    dma(g, "ldw", out=WB[:, :, :], in_=WvT_h.rearrange("(c p) e -> p c e", p=128))
    ldw_p1 = val("ldw")

    v.op("memset", KVACC[:, :], 0.0)
    v.wait("ldw", 16 * 5)
    v.op("memset", VAUG[:, :, 65:66], 0.0)
    v.op("memset", VAUG[:, :, 64:65], 1.0)
    vaug_ones = val("v2t")
    kvacc_zeroed = val("v2t")

    st_ = dict(
        kproj_free=[0, 0],
        xw_free=[0, 0],
        nsq_bank_free=0,
        kv_banks_free=0,
        kvacc_done=kvacc_zeroed,
        vaug_ready=0,
        phi_free=0,
        phi_free2=[0, 0],
        bia_ready=0,
        ensq_ready=0,
        nsq_evacd=0,
        apr_last=0,
        num_free=0,
        pqt_free=0,
        pqt_free2w=[0, 0],
        pbT_free=[0, 0],
        mrg_free=0,
        mgt_free=0,
        y_free=0,
        kvr_ready=0,
        xtm_free={},
        u_free=[0, 0],
        usb_free=[0, 0],
        r1nt_free=0,
        v_free=0,
        outs_free={},
        xb_ev={},
        xb_rel={},
    )

    def load_xb(tile, phase):
        buf = tile % 2
        key = None
        if phase == "p1" and tile >= 2:
            key = ("p1", tile - 2)
        elif phase == "p2":
            key = ("p1", 6 + tile) if tile < 2 else ("p2", tile - 2)
        if key is not None:
            sem, vv = st_["xb_rel"][key]
            g.wait(sem, vv)
        dma(g, "ldx", out=XB[:, buf, :, :],
            in_=xT.rearrange("(c p) t -> p c t", p=128)[:, :, tile * TS:(tile + 1) * TS])
        st_["xb_ev"][(phase, tile)] = val("ldx")

    def proj_nsq(tile, phase, ldw_cur, nsq_filler=None):
        buf = tile % 2
        t.wait("ldx", st_["xb_ev"][(phase, tile)])
        t.wait("ldw", ldw_cur)
        for c in range(8):
            bank = c % 2
            t.wait("s2v", max(st_["kproj_free"][bank], st_["xw_free"][bank]))
            for d in range(8):
                t.op("matmul", pb[bank][:, :], WA[:, d, c * 128:(c + 1) * 128],
                     XB[:, buf, d, :], start=(d == 0), stop=(d == 7),
                     inc=[("t2v", 1)] if d == 7 else ())
            ev = val("t2v")
            if phase == "p2" and c == 7:
                st_["xb_rel"][(phase, tile)] = ("t2v", ev)
            s.wait("t2v", ev)
            s.op("copy", out=KT[:, c, :], in_=pb[bank][:, :])
            st_["kproj_free"][bank] = val("s2v")
            v.wait("s2v", st_["kproj_free"][bank])
            v.op("tensor_mul", KSQ[:, c, :], KT[:, c, :], KT[:, c, :])
        ksq_ready = val("v2t")
        t.wait("v2t", ksq_ready)
        t.wait("v2t", st_["nsq_bank_free"])
        t.wait("v2t", st_["num_free"])
        first = True
        for su in range(4):
            for c in range(8):
                last = (su == 3 and c == 7)
                t.op("matmul", pb[2][:, 32 + su * 16:48 + su * 16],
                     KSQ[:, c, su * 128:(su + 1) * 128], IND[:, c, :],
                     start=first, stop=last,
                     inc=[("t2v", 1)] if last else ())
                first = False
            if nsq_filler is not None:
                nsq_filler(su)
        nsq_mm = val("t2v")
        v.wait("t2v", nsq_mm)
        v.op("tensor_copy", NSQ[:, :, :],
             pb[2].rearrange("p (a b) -> p a b", b=16)[:, 2:6, :])
        st_["nsq_evacd"] = val("v2t")
        v.wait("s2v", max(st_["ensq_ready"], st_["apr_last"]))
        vsync()
        v.op("tensor_scalar", out=BIA[:, :, :].rearrange("p s h -> p (s h)"),
             in0=NSQ[:, :, :].rearrange("p s h -> p (s h)"), scalar1=-0.5 / 64.0,
             scalar2=-LN16, op0=ALU.mult, op1=ALU.add)
        st_["bia_ready"] = val("v2t")
        s.wait("v2t", st_["bia_ready"])
        s.op("activation", out=ENSQ[:, :, :].rearrange("p s h -> p (s h)"),
             in_=BIA[:, :, :].rearrange("p s h -> p (s h)"), func=AF.Exp)
        st_["ensq_ready"] = val("s2v")

    def emit_xw_pair(su, p):
        bank = pb[p % 2]
        t.wait("s2v", max(st_["xw_free"][p % 2], max(st_["kproj_free"])))
        t.op("matmul", bank[:, 0:512], KT[:, p, su * 128:(su + 1) * 128],
             PRJBD[:, p, :], start=True, stop=True, inc=[("t2s", 1)])
        xw_ev = val("t2s")
        s.wait("t2s", xw_ev)
        s.wait("v2t", max(st_["bia_ready"], st_.get("apr_free_v2t", 0)))
        for h2 in range(2):
            h = 2 * p + h2
            s.op("activation", out=APR[:, h, :],
                 in_=bank[:, h2 * 256:(h2 + 1) * 256],
                 func=AF.Exp, bias=BIA[:, su, h:h + 1], scale=0.125)
        st_["xw_free"][p % 2] = val("s2v")
        if p == 7:
            st_["apr_last"] = val("s2v")

        PHI = PHIB[su % 2]
        pst = st_.setdefault("phipipe", {})

        def stage_reduce(q):
            sl = slice(2 * q, 2 * q + 2)
            v.wait("s2v", pst[("exps", su, q)])
            if q == 0:
                v.wait("s2v", st_["ensq_ready"])
                v.wait("t2v", st_["phi_free2"][su % 2])
            v.op("reduce_max", MAXA[:, su, sl], APR[:, sl, :], axis=AX.X)
            pst[("red", su, q)] = val("v2t")

        def stage_rest(q):
            sl = slice(2 * q, 2 * q + 2)
            v.wait("v2t", pst[("red", su, q)])
            v.op("reciprocal", SFT[:, su, sl], MAXA[:, su, sl])
            rc = val("v2t")
            v.wait("v2t", rc)
            v.op("tensor_mul", SFT[:, su, sl], SFT[:, su, sl], ENSQ[:, su, sl])
            mv = val("v2t")
            v.wait("v2t", mv)
            for h in (2 * q, 2 * q + 1):
                v.op("tensor_scalar", out=PHI[:, h, :], in0=APR[:, h, :],
                     scalar1=SFT[:, su, h:h + 1], scalar2=EPS16,
                     op0=ALU.mult, op1=ALU.add)

        pst[("exps", su, p)] = val("s2v")
        stage_reduce(p)
        if p > 0:
            stage_rest(p - 1)
        if p == 7:
            stage_rest(7)
            st_["apr_free_v2t"] = val("v2t")
        return val("v2t")

    def emit_phi_vec(su):
        return val("v2t")

    def emit_kv(su, h0, h1, phi_ready_su):
        firstk = (su == 0)
        t.wait("v2t", phi_ready_su)
        if h0 == 0:
            t.wait("s2v", st_["vaug_ready_kv"])
            if su == 0:
                t.wait("v2t", st_["kv_banks_free"])
                t.wait("v2t", st_["nsq_evacd"])
                t.wait("v2t", st_["nsq_bank_free"])
        PHI = PHIB[su % 2]
        for h in range(h0, h1):
            for c in range(2):
                idx = h * 2 + c
                if idx < 28:
                    bank, slot = KVB[idx // 7], idx % 7
                else:
                    bank, slot = pb[2], idx - 28
                stop_kv = (su == 3 and (slot == 6 or idx == 31))
                t.op("matmul", bank[:, slot * 65:slot * 65 + 65],
                     PHI[:, h, c * 128:(c + 1) * 128], VAUG[:, h, 0:65],
                     start=(firstk and slot == 0), stop=stop_kv,
                     inc=[("t2v", 1)] if (stop_kv or idx == 31) else ())
        if h1 == H:
            st_["phi_free2"][su % 2] = val("t2v")

    st_["vaug_ready_kv"] = 0
    for tile in range(NT):
        load_xb(tile, "p1")
        proj_nsq(tile, "p1", ldw_p1)
        prev_phi_ready = st_.get("p1_phi_ready", 0)
        for su in range(4):
            buf = tile % 2
            for p in range(8):
                ph = emit_xw_pair(su, p)
                if su > 0 and p < 4:
                    emit_kv(su - 1, p * 4, (p + 1) * 4, prev_phi_ready)
            cur_phi_ready = ph
            for gr in range(2):
                bank = gr
                t.wait("s2v", max(st_["xw_free"][bank], st_["kproj_free"][bank]))
                for d in range(8):
                    lastd = (d == 7)
                    t.op("matmul", pb[bank][:, :],
                         XB[:, buf, d, su * 128:(su + 1) * 128],
                         WB[:, d, gr * 512:(gr + 1) * 512],
                         start=(d == 0), stop=lastd,
                         inc=[("t2v", 1)] if lastd else ())
                ev = val("t2v")
                if su == 3 and gr == 1:
                    st_["xb_rel"][("p1", tile)] = ("t2v", ev)
                s.wait("t2v", ev)
                s.wait("t2v", st_["phi_free2"][(su - 1) % 2])
                if tile == 0 and su == 0 and gr == 0:
                    s.wait("v2t", vaug_ones)
                s.op("copy", out=VAUG[:, gr * 8:(gr + 1) * 8, 0:64],
                     in_=pb[bank].rearrange("p (h e) -> p h e", h=8)[:, :, :])
                st_["kproj_free"][bank] = val("s2v")
            st_["vaug_ready_kv"] = val("s2v")
            prev_phi_ready = cur_phi_ready
            st_["p1_phi_ready"] = prev_phi_ready
        emit_kv(3, 0, H, prev_phi_ready)
        kv_done = val("t2v")
        v.wait("t2v", kv_done)
        v.wait("v2t", st_["kvacc_done"])
        for b in range(4):
            v.op("tensor_add", KVACC[:, b * 455:(b + 1) * 455],
                 KVACC[:, b * 455:(b + 1) * 455], KVB[b][:, 0:455])
        v.op("tensor_add", KVACC[:, 1820:2080], KVACC[:, 1820:2080], pb[2][:, 0:260])
        st_["kvacc_done"] = val("v2t")
        st_["kv_banks_free"] = val("v2t")
        st_["nsq_bank_free"] = val("v2t")

    vsync()
    v.op("tensor_copy", KVR[:, :, :, 0:65],
         KVACC.rearrange("p (h c e) -> p h c e", h=16, c=2))
    v.op("memset", KVR[:, :, :, 65:66], 0.0)
    st_["kvr_ready"] = val("v2t")
    if debug:
        vsync()
        v.op("tensor_copy", R1.rearrange("p (i j) -> p i j", j=64),
             KVACC.rearrange("p (i j) -> p i j", j=65)[:, 0:16, 0:64])
        vsync()
        sy.wait("v2t", val("v2t"))
        dma(sy, "st", out=dbg["kv"][:, 0:1024], in_=R1[:, :])
        v.wait("st", val("st"))
        v.op("tensor_copy", R1.rearrange("p (i j) -> p i j", j=64),
             KVACC.rearrange("p (i j) -> p i j", j=65)[:, 16:32, 0:64])
        vsync()
        sy.wait("v2t", val("v2t"))
        dma(sy, "st", out=dbg["kv"][:, 1024:2048], in_=R1[:, :])
        v.wait("st", val("st"))
        v.op("tensor_copy", R1[:, 0:32].rearrange("p (i j) -> p i j", j=1),
             KVACC.rearrange("p (i j) -> p i j", j=65)[:, :, 64:65])
        vsync()
        sy.wait("v2t", val("v2t"))
        dma(sy, "st", out=dbg["ksum"][:, :], in_=R1[:, 0:32])
        v.wait("st", val("st"))

    pb6_cm.__exit__(None, None, None)
    pbT2 = nc.psum_tensor("pbT2", [128, 1024], BF16)
    pbT2t = pbT2.__enter__()
    ctx.append(pbT2)
    pbTs = [pbT, pbT2t]

    def phase_reset():
        st_["kproj_free"] = [val("s2v"), val("s2v")]
        st_["xw_free"] = [val("s2v"), val("s2v")]
        st_["nsq_bank_free"] = val("v2t")
        st_["num_free"] = val("v2t")
        st_["y_free"] = val("v2t")
        st_["v_free"] = val("v2t")
        st_["mgt_free"] = val("t2v")
        st_["mrg_free"] = val("t2v")
        st_["pbT_free"] = [val("v2t"), val("v2t")]
        st_["pqt_free"] = val("t2v")
        st_["pqt_free2w"] = [val("t2v"), val("t2v")]
        st_["phi_free"] = val("t2v")
        st_["phi_free2"] = [val("t2v"), val("t2v")]
        st_["u_free"] = [val("s2v"), val("s2v")]
        st_["usb_free"] = [val("t2s"), val("t2s")]
        st_["r1nt_free"] = val("t2s")

    def phase_barrier():
        g.wait("t2v", val("t2v"))
        g.wait("t2s", val("t2s"))
        g.wait("v2t", val("v2t"))
        g.wait("s2v", val("s2v"))

    for half in range(2):
        phase_barrier()
        dma(g, "ldw", out=WA[:, :, :], in_=WqT_h.rearrange("(c p) e -> p c e", p=128))
        dma(g, "ldw", out=WB[:, :, :], in_=WoT_h.rearrange("(c p) e -> p c e", p=128))
        ldw_p2 = val("ldw")
        phase_reset()

        for tile in range(half * 4, half * 4 + 4):
            load_xb(tile, "p2")
            proj_nsq(tile, "p2", ldw_p2)
            pend = []
            for p in range(8):
                st_["p2_phi_ready"] = emit_xw_pair(0, p)
            for su in range(4):
                sub = tile * 4 + su
                rsub = sub % 16
                if sub >= 1:
                    sy.wait("v2t", st_["xtm_free"].get(sub - 1, 0))
                dma(sy, "ldy", out=XTM[:, :],
                    in_=x_tm[sub * 128:(sub + 1) * 128, :])
                xtm_ev = val("ldy")
                phi_ready = st_["p2_phi_ready"]
                pend = list(range(8)) if su < 3 else []

                def drip(k):
                    for _ in range(k):
                        if pend:
                            st_["p2_phi_ready"] = emit_xw_pair(su + 1, pend.pop(0))

                PHI = PHIB[su % 2]
                t.wait("v2t", st_["kvr_ready"])
                wave_trev = {}
                wave_evac = {}
                grp_start_done = [False, False]

                def emit_T(w):
                    bnk = w % 2
                    t.wait("v2t", phi_ready)
                    t.wait("v2t", st_["pbT_free"][bnk])
                    t.wait("v2t", st_["num_free"] if w < 2 else 0)
                    for hh4 in range(4):
                        h = w * 4 + hh4
                        for c in range(2):
                            slot = hh4 * 2 + c
                            t.op("transpose", pbTs[bnk][:, slot * 128:(slot + 1) * 128],
                                 PHI[:, h, c * 128:(c + 1) * 128], IDN[:, :],
                                 inc=[("t2v", 1)])
                    wave_trev[w] = val("t2v")
                    if w == 3:
                        st_["phi_free2"][su % 2] = wave_trev[w]

                def emit_evac(w):
                    bnk = w % 2
                    v.wait("t2v", wave_trev[w])
                    v.wait("t2v", st_["pqt_free2w"][w % 2])
                    v.op("tensor_copy", PQT8[:, (w % 2) * 1024:(w % 2) * 1024 + 1024],
                         pbTs[bnk][:, :])
                    st_["pbT_free"][bnk] = val("v2t")
                    wave_evac[w] = val("v2t")

                def emit_num(w):
                    grp = w // 2
                    t.wait("v2t", wave_evac[w])
                    if w % 2 == 0:
                        t.wait("v2t", st_["num_free"])
                    for hh4 in range(4):
                        hh = (w % 2) * 4 + hh4
                        h = w * 4 + hh4
                        nb = NUMB[w % 2]
                        nslot = hh4
                        base = (w % 2) * 1024 + hh4 * 256
                        for c in range(2):
                            t.op("matmul", nb[:, nslot * 128:nslot * 128 + 66],
                                 PQT8[:, base + c * 128:base + c * 128 + 128],
                                 KVR[:, h, c, :],
                                 start=(c == 0 and hh4 == 0),
                                 stop=(nslot == 3 and c == 1),
                                 inc=[("t2v", 1)] if c == 1 else ())
                    st_["pqt_free2w"][w % 2] = val("t2v")

                emit_T(0)
                emit_evac(0)
                for w in range(4):
                    if w < 3:
                        emit_T(w + 1)
                        emit_evac(w + 1)
                    emit_num(w)
                    drip(1)
                    if w % 2 == 1:
                        numg = val("t2v")
                        v.wait("t2v", numg)
                        base_h = (w - 1) * 4
                        for bb in range(2):
                            v.op("tensor_scalar",
                                 out=DEN[:, base_h + bb * 4:base_h + bb * 4 + 4],
                                 in0=NUMB[bb].rearrange("p (a b) -> p a b", b=128)[:, :, 64:65]
                                 .rearrange("p a b -> p (a b)"),
                                 scalar1=EPS, scalar2=None, op0=ALU.add)
                        vsync()
                        v.op("reciprocal", RDEN[:, base_h:base_h + 8],
                             DEN[:, base_h:base_h + 8])
                        vsync()
                        v.wait("t2v", st_["mrg_free"])
                        for hh in range(8):
                            h = base_h + hh
                            v.op("tensor_scalar", out=MRG[:, h * 64:(h + 1) * 64],
                                 in0=NUMB[hh // 4][:, (hh % 4) * 128:(hh % 4) * 128 + 64],
                                 scalar1=RDEN[:, h:h + 1], scalar2=None, op0=ALU.mult)
                        st_["num_free"] = val("v2t")
                        drip(1)
                mrg_ready = val("v2t")
                t.wait("v2t", mrg_ready)
                t.wait("v2t", st_["pbT_free"][0])
                for c in range(8):
                    t.op("transpose", pbTs[0][:, c * 128:(c + 1) * 128],
                         MRG[:, c * 128:(c + 1) * 128], IDN[:, :],
                         inc=[("t2v", 1)])
                st_["mrg_free"] = val("t2v")
                drip(2)
                v.wait("t2v", st_["mrg_free"])
                v.wait("t2v", st_["mgt_free"])
                v.op("tensor_copy", MGT[:, :, :], pbTs[0].rearrange("p (c t) -> p c t", c=8))
                st_["pbT_free"][0] = val("v2t")
                mgt_ready = val("v2t")
                t.wait("v2t", mgt_ready)
                t.wait("v2t", st_["y_free"])
                for gr in range(2):
                    for ic in range(8):
                        t.op("matmul", YB[gr][:, :], MGT[:, ic, :],
                             WB[:, ic, gr * 512:(gr + 1) * 512],
                             start=(ic == 0), stop=False)
                    t.op("matmul", YB[gr][:, :], ONESR[:, :],
                         BOROW[:, gr * 512:(gr + 1) * 512],
                         start=False, stop=True, inc=[("t2v", 1)])
                st_["mgt_free"] = val("t2v")
                yev = val("t2v")
                drip(8)
                v.wait("t2v", yev)
                v.wait("ldy", xtm_ev)
                vsync()
                for gr in range(2):
                    v.op("scalar_tensor_tensor", out=R1[:, gr * 512:(gr + 1) * 512],
                         in0=YB[gr][:, :], scalar=1.0,
                         in1=XTM[:, gr * 512:(gr + 1) * 512],
                         op0=ALU.mult, op1=ALU.add)
                st_["y_free"] = val("v2t")
                st_["xtm_free"][sub] = val("v2t")
                vsync()
                v.op("reduce_sum", MU[:, 0:1], R1[:, :], axis=AX.X)
                vsync()
                v.op("tensor_scalar", out=MU[:, 2:3], in0=MU[:, 0:1], scalar1=1.0 / D,
                     scalar2=None, op0=ALU.mult)
                vsync()
                v.op("scalar_tensor_tensor", out=JNK[:, :], in0=R1[:, :],
                     scalar=MU[:, 2:3], in1=R1[:, :], op0=ALU.subtract, op1=ALU.mult,
                     accum_out=MU[:, 1:2])
                vsync()
                v.op("tensor_scalar", out=MU[:, 3:4], in0=MU[:, 1:2], scalar1=1.0 / D,
                     scalar2=LNEPS, op0=ALU.mult, op1=ALU.add)
                var_ev = val("v2t")
                s.wait("v2t", var_ev)
                s.op("activation", out=MU[:, 4:5], in_=MU[:, 3:4], func=AF.Ln)
                s.wait("s2v", val("s2v"))
                s.op("activation", out=MU[:, 5:6], in_=MU[:, 4:5], func=AF.Exp,
                     scale=-0.5)
                sq_ev = val("s2v")
                v.wait("s2v", sq_ev)
                v.op("tensor_scalar", out=r1n[:, rsub, :], in0=R1[:, :],
                     scalar1=MU[:, 2:3], scalar2=MU[:, 5:6],
                     op0=ALU.subtract, op1=ALU.mult)

        if debug:
            for rsub in range(16):
                sub = half * 16 + rsub
                v.wait("st", val("st"))
                vsync()
                v.op("tensor_copy", R1[:, :], r1n[:, rsub, :])
                vsync()
                sy.wait("v2t", val("v2t"))
                dma(sy, "st", out=dbg["r1n"][sub * 128:(sub + 1) * 128, :],
                    in_=R1[:, :])
            v.wait("st", val("st"))

        phase_barrier()
        ldw_w1q = []
        ldw_w2q = []
        for q in range(4):
            dma(g, "ldw", out=W1V[:, :, q * 1024:(q + 1) * 1024],
                in_=W1_h.rearrange("(c p) e -> p c e", p=128)[:, :, q * 1024:(q + 1) * 1024])
            ldw_w1q.append(val("ldw"))
            dma(g, "ldw", out=W2V[:, q * 8:(q + 1) * 8, :],
                in_=W2_h.rearrange("(c p) e -> p c e", p=128)[:, q * 8:(q + 1) * 8, :])
            ldw_w2q.append(val("ldw"))
        ldw_p3 = val("ldw")
        phase_reset()

        for tile in range(half * 4, half * 4 + 4):
            for pr in range(2):
                t.wait("v2t", st_["v_free"])
                t.wait("t2s", st_["r1nt_free"])
                for s2 in range(2):
                    sub = tile * 4 + pr * 2 + s2
                    rsub = sub % 16
                    t.wait("v2t", st_["pbT_free"][s2])
                    for c in range(8):
                        t.op("transpose", pbTs[s2][:, c * 128:(c + 1) * 128],
                             r1n[:, rsub, c * 128:(c + 1) * 128], IDN[:, :],
                             inc=[("t2v", 1)])
                    ev = val("t2v")
                    v.wait("t2v", ev)
                    v.wait("t2s", st_["r1nt_free"])
                    v.op("tensor_copy", R1NT[:, :, s2 * 128:(s2 + 1) * 128],
                         pbTs[s2].rearrange("p (c t) -> p c t", c=8))
                    st_["pbT_free"][s2] = val("v2t")
                r1nt_ready = val("v2t")
                t.wait("v2t", r1nt_ready)

                def emit_u(ic):
                    ub = ic % 2
                    t.wait("ldw", ldw_w1q[ic // 8])
                    t.wait("s2v", st_["u_free"][ub])
                    for d in range(8):
                        t.op("matmul", pb[ub][:, 0:256],
                             W1V[:, d, ic * 128:(ic + 1) * 128], R1NT[:, d, :],
                             start=(d == 0), stop=(d == 7),
                             inc=[("t2s", 1)] if d == 7 else ())
                    uev = val("t2s")
                    if ic == 31:
                        st_["r1nt_free"] = uev
                    s.wait("t2s", uev)
                    s.wait("t2s", st_["usb_free"][ub])
                    s.op("activation", out=USB[:, ub, :], in_=pb[ub][:, 0:256],
                         func=AF.Relu, bias=B1COL[:, ic:ic + 1])
                    st_["u_free"][ub] = val("s2v")

                def emit_v(ic):
                    ub = ic % 2
                    t.wait("ldw", ldw_w2q[ic // 8])
                    t.wait("s2v", st_["u_free"][ub])
                    for s2 in range(2):
                        for gr in range(2):
                            vb = VB[s2 * 2 + gr]
                            t.op("matmul", vb[:, :], USB[:, ub, s2 * 128:(s2 + 1) * 128],
                                 W2V[:, ic, gr * 512:(gr + 1) * 512],
                                 start=(ic == 0), stop=False,
                                 inc=[("t2s", 1)] if (s2 == 1 and gr == 1) else ())
                    st_["usb_free"][ub] = val("t2s")

                emit_u(0)
                for ic in range(1, 32):
                    emit_u(ic)
                    emit_v(ic - 1)
                emit_v(31)
                for s2 in range(2):
                    for gr in range(2):
                        vb = VB[s2 * 2 + gr]
                        t.op("matmul", vb[:, :], ONESR[:, :],
                             B2ROW[:, gr * 512:(gr + 1) * 512],
                             start=False, stop=True, inc=[("t2v", 1)])
                vev = val("t2v")
                v.wait("t2v", vev)
                for s2 in range(2):
                    sub = tile * 4 + pr * 2 + s2
                    rsub = sub % 16
                    ob = sub % 2
                    v.wait("st", st_["outs_free"].get(sub - 2, 0))
                    vsync()
                    for gr in range(2):
                        v.op("scalar_tensor_tensor", out=R1[:, gr * 512:(gr + 1) * 512],
                             in0=VB[s2 * 2 + gr][:, :], scalar=1.0,
                             in1=r1n[:, rsub, gr * 512:(gr + 1) * 512],
                             op0=ALU.mult, op1=ALU.add)
                    if s2 == 1:
                        st_["v_free"] = val("v2t")
                    vsync()
                    v.op("reduce_sum", MU[:, 0:1], R1[:, :], axis=AX.X)
                    vsync()
                    v.op("tensor_scalar", out=MU[:, 2:3], in0=MU[:, 0:1],
                         scalar1=1.0 / D, scalar2=None, op0=ALU.mult)
                    vsync()
                    v.op("scalar_tensor_tensor", out=OUTS[:, ob, :], in0=R1[:, :],
                         scalar=MU[:, 2:3], in1=R1[:, :],
                         op0=ALU.subtract, op1=ALU.mult, accum_out=MU[:, 1:2])
                    vsync()
                    v.op("tensor_scalar", out=MU[:, 3:4], in0=MU[:, 1:2],
                         scalar1=1.0 / D, scalar2=LNEPS, op0=ALU.mult, op1=ALU.add)
                    var_ev = val("v2t")
                    s.wait("v2t", var_ev)
                    s.op("activation", out=MU[:, 4:5], in_=MU[:, 3:4], func=AF.Ln)
                    s.wait("s2v", val("s2v"))
                    s.op("activation", out=MU[:, 5:6], in_=MU[:, 4:5], func=AF.Exp,
                         scale=-0.5)
                    sq_ev = val("s2v")
                    v.wait("s2v", sq_ev)
                    v.op("tensor_scalar", out=OUTS[:, ob, :], in0=R1[:, :],
                         scalar1=MU[:, 2:3], scalar2=MU[:, 5:6],
                         op0=ALU.subtract, op1=ALU.mult)
                    oev = val("v2t")
                    sy.wait("v2t", oev)
                    dma(sy, "st", out=out_h[sub * 128:(sub + 1) * 128, :],
                        in_=OUTS[:, ob, :])
                    st_["outs_free"][sub] = val("st")

    sy.wait("st", val("st"))
    g.wait("ldw", val("ldw"))
    g.wait("ldx", val("ldx"))

    blk_cm = nc.Block()
    blk = blk_cm.__enter__()

    def emitter(ename):
        def _(eng):
            for f in ops[ename]:
                f(eng)
        return _
    blk.tensor(emitter("tensor"))
    blk.vector(emitter("vector"))
    blk.scalar(emitter("scalar"))
    blk.sync(emitter("sync"))
    blk.gpsimd(emitter("gpsimd"))
    blk_cm.__exit__(None, None, None)
    for h in reversed(ctx):
        h.__exit__(None, None, None)
    return nc


def make_inputs(x_b, Wq, Wk, Wv, Wo, bo, proj, W1, b1, W2, b2):
    c = np.ascontiguousarray
    ind = np.zeros((8, 128, H), np.float32)
    for ch in range(8):
        ind[ch, 0:64, 2 * ch] = 1.0
        ind[ch, 64:128, 2 * ch + 1] = 1.0
    pt = np.asarray(proj).transpose(0, 2, 1)
    projBD = np.zeros((8, 128, 512), np.float32)
    projBD[:, 0:64, 0:256] = pt[0::2]
    projBD[:, 64:128, 256:512] = pt[1::2]
    return {
        "xT": c(np.asarray(x_b).T),
        "x": c(np.asarray(x_b)),
        "WqT": c(np.asarray(Wq).T), "WkT": c(np.asarray(Wk).T),
        "WvT": c(np.asarray(Wv).T), "WoT": c(np.asarray(Wo).T),
        "W1": c(np.asarray(W1)), "W2": c(np.asarray(W2)),
        "projBD": c(projBD),
        "ind": ind,
        "ident": np.eye(128, dtype=np.float32),
        "onesrow": np.ones((1, 128), np.float32),
        "onescol": np.ones((128, 1), np.float32),
        "borow": c(np.asarray(bo).reshape(1, D)),
        "b2row": c(np.asarray(b2).reshape(1, D)),
        "b1col": c(np.asarray(b1).reshape(32, 128).T),
    }



from concourse.bass_utils import run_bass_kernel_spmd


def kernel(L0_attn_x, L0_attn_Wq, L0_attn_Wk, L0_attn_Wv, L0_attn_Wo, L0_attn_bo,
           L0_attn_proj, L0_ffn_W1, L0_ffn_b1, L0_ffn_W2, L0_ffn_b2):
    x = np.asarray(L0_attn_x)
    nc = build_program(debug=False)
    in_maps = []
    for b in range(8):
        in_maps.append(make_inputs(x[b], L0_attn_Wq, L0_attn_Wk, L0_attn_Wv,
                                   L0_attn_Wo, L0_attn_bo, L0_attn_proj,
                                   L0_ffn_W1, L0_ffn_b1, L0_ffn_W2, L0_ffn_b2))
    res = run_bass_kernel_spmd(nc, in_maps, core_ids=list(range(8)))
    out = np.stack([np.asarray(res.results[b]["out"]) for b in range(8)], axis=0)
    return out.astype(np.float32)


# version 11
# speedup vs baseline: 1.1083x; 1.0268x over previous
import numpy as np
import concourse.bass as bass
import concourse.mybir as mybir

F32 = mybir.dt.float32
BF16 = mybir.dt.bfloat16
AF = mybir.ActivationFunctionType
ALU = mybir.AluOpType
AX = mybir.AxisListType

T, D, H, DK, M, INTER = 4096, 1024, 16, 64, 256, 4096
NT, TS, NSUB = 8, 512, 4
EPS = 1e-6
EPS16 = EPS / 16.0
LN16 = float(np.log(16.0))
LNEPS = 1e-5
ENGS = ("tensor", "vector", "scalar", "sync", "gpsimd")


def build_program(debug=False):
    nc = bass.Bass("TRN2", target_bir_lowering=False)
    din = lambda n, s, dt=F32: nc.dram_tensor(n, s, dt, kind="ExternalInput").ap()
    dout = lambda n, s, dt=F32: nc.dram_tensor(n, s, dt, kind="ExternalOutput").ap()

    xT = din("xT", [D, T])
    x_tm = din("x", [T, D])
    WqT_h = din("WqT", [D, D])
    WkT_h = din("WkT", [D, D])
    WvT_h = din("WvT", [D, D])
    WoT_h = din("WoT", [D, D])
    W1_h = din("W1", [D, INTER])
    W2_h = din("W2", [INTER, D])
    projBD_h = din("projBD", [8, 128, 512])
    ind_h = din("ind", [8, 128, H])
    ident_h = din("ident", [128, 128])
    onesrow_h = din("onesrow", [1, 128])
    onescol_h = din("onescol", [128, 1])
    borow_h = din("borow", [1, D])
    b2row_h = din("b2row", [1, D])
    b1col_h = din("b1col", [128, 32])
    out_h = dout("out", [T, D])
    dbg = {}
    if debug:
        dbg["kv"] = dout("dbg_kv", [128, 2048])
        dbg["ksum"] = dout("dbg_ksum", [128, 32])
        dbg["phik"] = dout("dbg_phik", [128, H * M])
        dbg["r1n"] = dout("dbg_r1n", [T, D])

    ctx = []

    def sb(name, shape, dt):
        h = nc.sbuf_tensor(name, shape, dt)
        t_ = h.__enter__()
        ctx.append(h)
        return t_

    def psum(name, shape, dt):
        h = nc.psum_tensor(name, shape, dt)
        t_ = h.__enter__()
        ctx.append(h)
        return t_

    mega = sb("mega", [128, 64 * 1024], BF16)
    W1V = mega[:, 0:32768].rearrange("p (c e) -> p c e", c=8)
    W2V = mega[:, 32768:65536].rearrange("p (c e) -> p c e", c=32)
    WA = mega[:, 0:8192].rearrange("p (c e) -> p c e", c=8)
    WB = mega[:, 8192:16384].rearrange("p (c e) -> p c e", c=8)
    XB = mega[:, 16384:24576].rearrange("p (b c t) -> p b c t", b=2, c=8)
    KT = mega[:, 24576:28672].rearrange("p (c t) -> p c t", c=8)
    KSQ = mega[:, 28672:32768].rearrange("p (c t) -> p c t", c=8)
    APR = mega[:, 32768:36864].rearrange("p (h m) -> p h m", h=H)
    PHI0 = mega[:, 36864:40960].rearrange("p (h m) -> p h m", h=H)
    PHI1 = mega[:, 47648:51744].rearrange("p (h m) -> p h m", h=H)
    PHIB = [PHI0, PHI1]
    VAUG = mega[:, 40960:42016].rearrange("p (h j) -> p h j", h=H)
    MRG = mega[:, 42016:43040]
    PQT = mega[:, 43040:43552].rearrange("p (b m) -> p b m", b=2)
    PQT8 = mega[:, 45600:47648]
    MGT = mega[:, 43552:44576].rearrange("p (c t) -> p c t", c=8)
    JNK = mega[:, 44576:45600]

    fpool = sb("fpool", [128, 6568], F32)
    KVACC = fpool[:, 0:2080]
    NSQ = fpool[:, 2112:2176].rearrange("p (s h) -> p s h", s=4)
    BIA = fpool[:, 2176:2240].rearrange("p (s h) -> p s h", s=4)
    ENSQ = fpool[:, 2240:2304].rearrange("p (s h) -> p s h", s=4)
    MAXA = fpool[:, 2304:2368].rearrange("p (s h) -> p s h", s=4)
    SFT = fpool[:, 2368:2432].rearrange("p (s h) -> p s h", s=4)
    DEN = fpool[:, 2432:2448]
    RDEN = fpool[:, 2448:2464]
    R1 = fpool[:, 2464:3488]
    XTM = fpool[:, 3488:4512]
    MU = fpool[:, 4512:4520]
    OUTS = fpool[:, 4520:6568].rearrange("p (b e) -> p b e", b=2)

    r1n = sb("r1n", [128, 16, 1024], BF16)
    KVR = sb("kvr", [128, 16, 2, 66], BF16)
    R1NT = sb("r1nt", [128, 8, 256], BF16)
    USB = sb("usb", [128, 2, 256], BF16)
    IDN = sb("idn", [128, 128], BF16)
    IND = sb("indb", [128, 8, 16], BF16)
    PRJBD = sb("prjbd", [128, 8, 512], BF16)
    ONESR = sb("onesr", [1, 128], BF16)
    ONESC = sb("onesc", [128, 1], BF16)
    BOROW = sb("borow_sb", [1, 1024], BF16)
    B2ROW = sb("b2row_sb", [1, 1024], BF16)
    B1COL = sb("b1col_sb", [128, 32], F32)

    pb = [psum(f"pb{i}", [128, 512], F32) for i in range(6)]
    pbT = psum("pbT", [128, 1024], BF16)
    pb6_cm = nc.psum_tensor("pb6", [128, 512], F32)
    pb6 = pb6_cm.__enter__()
    KVB = [pb[3], pb[4], pb[5], pb6]
    NUMB = [pb[2], pb[3]]
    YB = [pb[4], pb[5]]
    VB = [pb[2], pb[3], pb[4], pb[5]]

    sems = {}
    cnt = {}
    for s_ in ("ldw", "ldx", "ldy", "st", "t2v", "t2s", "v2t", "s2v"):
        h = nc.semaphore(s_)
        sems[s_] = h.__enter__()
        ctx.append(h)
        cnt[s_] = 0
    ops = {e: [] for e in ENGS}

    class Rec:
        def __init__(self, eng, auto=None):
            self.e = eng
            self.auto = auto

        def wait(self, sem, v):
            if v > 0:
                ops[self.e].append(lambda g, s=sems[sem], v=v: g.wait_ge(s, v))

        def op(self, meth, *a, inc=(), **kw):
            inc = list(inc)
            if self.auto is not None and meth != "dma_start" and not inc:
                inc = [(self.auto, 1)]
            assert len(inc) <= 1, inc
            incs = []
            for sname, n in inc:
                cnt[sname] += n
                incs.append((sems[sname], n))

            def f(g, meth=meth, a=a, kw=kw, incs=incs):
                r = getattr(g, meth)(*a, **kw)
                for s_, n_ in incs:
                    r = r.then_inc(s_, n_)
            ops[self.e].append(f)

    t = Rec("tensor")
    v = Rec("vector", auto="v2t")
    s = Rec("scalar", auto="s2v")
    sy = Rec("sync")
    g = Rec("gpsimd")
    val = lambda sem: cnt[sem]

    def dma(rec, sem, **kw):
        rec.wait(sem, val(sem))
        rec.op("dma_start", inc=[(sem, 16)], **kw)

    def vsync():
        v.wait("v2t", val("v2t"))

    dma(g, "ldw", out=IDN[:, :], in_=ident_h[:, :])
    dma(g, "ldw", out=IND[:, :, :], in_=ind_h.rearrange("c p h -> p c h"))
    dma(g, "ldw", out=PRJBD[:, :, :], in_=projBD_h.rearrange("c p m -> p c m"))
    dma(g, "ldw", out=ONESR[:, :], in_=onesrow_h[:, :])
    dma(g, "ldw", out=ONESC[:, :], in_=onescol_h[:, :])
    dma(g, "ldw", out=BOROW[:, :], in_=borow_h[:, :])
    dma(g, "ldw", out=B2ROW[:, :], in_=b2row_h[:, :])
    dma(g, "ldw", out=B1COL[:, :], in_=b1col_h[:, :])
    dma(g, "ldw", out=WA[:, :, :], in_=WkT_h.rearrange("(c p) e -> p c e", p=128))
    dma(g, "ldw", out=WB[:, :, :], in_=WvT_h.rearrange("(c p) e -> p c e", p=128))
    ldw_p1 = val("ldw")

    v.op("memset", KVACC[:, :], 0.0)
    v.wait("ldw", 16 * 5)
    v.op("memset", VAUG[:, :, 65:66], 0.0)
    v.op("memset", VAUG[:, :, 64:65], 1.0)
    vaug_ones = val("v2t")
    kvacc_zeroed = val("v2t")

    st_ = dict(
        kproj_free=[0, 0],
        xw_free=[0, 0],
        nsq_bank_free=0,
        kv_banks_free=0,
        kvacc_done=kvacc_zeroed,
        vaug_ready=0,
        phi_free=0,
        phi_free2=[0, 0],
        bia_ready=0,
        ensq_ready=0,
        nsq_evacd=0,
        apr_last=0,
        num_free=0,
        pqt_free=0,
        pqt_free2w=[0, 0],
        pbT_free=[("v2t", 0), ("v2t", 0)],
        mrg_free=0,
        mgt_free=0,
        y_free=0,
        kvr_ready=0,
        xtm_free={},
        u_free=[0, 0],
        usb_free=[0, 0],
        r1nt_free=0,
        v_free=0,
        outs_free={},
        xb_ev={},
        xb_rel={},
    )

    def load_xb(tile, phase):
        buf = tile % 2
        key = None
        if phase == "p1" and tile >= 2:
            key = ("p1", tile - 2)
        elif phase == "p2":
            key = ("p1", 6 + tile) if tile < 2 else ("p2", tile - 2)
        if key is not None:
            sem, vv = st_["xb_rel"][key]
            g.wait(sem, vv)
        dma(g, "ldx", out=XB[:, buf, :, :],
            in_=xT.rearrange("(c p) t -> p c t", p=128)[:, :, tile * TS:(tile + 1) * TS])
        st_["xb_ev"][(phase, tile)] = val("ldx")

    def proj_nsq(tile, phase, ldw_cur, nsq_filler=None):
        buf = tile % 2
        t.wait("ldx", st_["xb_ev"][(phase, tile)])
        t.wait("ldw", ldw_cur)
        for c in range(8):
            bank = c % 2
            t.wait("s2v", max(st_["kproj_free"][bank], st_["xw_free"][bank]))
            for d in range(8):
                t.op("matmul", pb[bank][:, :], WA[:, d, c * 128:(c + 1) * 128],
                     XB[:, buf, d, :], start=(d == 0), stop=(d == 7),
                     inc=[("t2v", 1)] if d == 7 else ())
            ev = val("t2v")
            if phase == "p2" and c == 7:
                st_["xb_rel"][(phase, tile)] = ("t2v", ev)
            s.wait("t2v", ev)
            s.op("copy", out=KT[:, c, :], in_=pb[bank][:, :])
            st_["kproj_free"][bank] = val("s2v")
            v.wait("s2v", st_["kproj_free"][bank])
            v.op("tensor_mul", KSQ[:, c, :], KT[:, c, :], KT[:, c, :])
        ksq_ready = val("v2t")
        t.wait("v2t", ksq_ready)
        t.wait("v2t", st_["nsq_bank_free"])
        t.wait("v2t", st_["num_free"])
        first = True
        for su in range(4):
            for c in range(8):
                last = (su == 3 and c == 7)
                t.op("matmul", pb[2][:, 32 + su * 16:48 + su * 16],
                     KSQ[:, c, su * 128:(su + 1) * 128], IND[:, c, :],
                     start=first, stop=last,
                     inc=[("t2v", 1)] if last else ())
                first = False
            if nsq_filler is not None:
                nsq_filler(su)
        nsq_mm = val("t2v")
        v.wait("t2v", nsq_mm)
        v.op("tensor_copy", NSQ[:, :, :],
             pb[2].rearrange("p (a b) -> p a b", b=16)[:, 2:6, :])
        st_["nsq_evacd"] = val("v2t")
        v.wait("s2v", max(st_["ensq_ready"], st_["apr_last"]))
        vsync()
        v.op("tensor_scalar", out=BIA[:, :, :].rearrange("p s h -> p (s h)"),
             in0=NSQ[:, :, :].rearrange("p s h -> p (s h)"), scalar1=-0.5 / 64.0,
             scalar2=-LN16, op0=ALU.mult, op1=ALU.add)
        st_["bia_ready"] = val("v2t")
        s.wait("v2t", st_["bia_ready"])
        s.op("activation", out=ENSQ[:, :, :].rearrange("p s h -> p (s h)"),
             in_=BIA[:, :, :].rearrange("p s h -> p (s h)"), func=AF.Exp)
        st_["ensq_ready"] = val("s2v")

    def emit_xw_pair(su, p):
        bank = pb[p % 2]
        t.wait("s2v", max(st_["xw_free"][p % 2], max(st_["kproj_free"])))
        t.op("matmul", bank[:, 0:512], KT[:, p, su * 128:(su + 1) * 128],
             PRJBD[:, p, :], start=True, stop=True, inc=[("t2s", 1)])
        xw_ev = val("t2s")
        s.wait("t2s", xw_ev)
        s.wait("v2t", max(st_["bia_ready"], st_.get("apr_free_v2t", 0)))
        for h2 in range(2):
            h = 2 * p + h2
            s.op("activation", out=APR[:, h, :],
                 in_=bank[:, h2 * 256:(h2 + 1) * 256],
                 func=AF.Exp, bias=BIA[:, su, h:h + 1], scale=0.125)
        st_["xw_free"][p % 2] = val("s2v")
        if p == 7:
            st_["apr_last"] = val("s2v")

        PHI = PHIB[su % 2]
        pst = st_.setdefault("phipipe", {})

        def stage_reduce(q):
            sl = slice(2 * q, 2 * q + 2)
            v.wait("s2v", pst[("exps", su, q)])
            if q == 0:
                v.wait("s2v", st_["ensq_ready"])
                v.wait("t2v", st_["phi_free2"][su % 2])
            v.op("reduce_max", MAXA[:, su, sl], APR[:, sl, :], axis=AX.X)
            pst[("red", su, q)] = val("v2t")

        def stage_rest(q):
            sl = slice(2 * q, 2 * q + 2)
            v.wait("v2t", pst[("red", su, q)])
            v.op("reciprocal", SFT[:, su, sl], MAXA[:, su, sl])
            rc = val("v2t")
            v.wait("v2t", rc)
            v.op("tensor_mul", SFT[:, su, sl], SFT[:, su, sl], ENSQ[:, su, sl])
            mv = val("v2t")
            v.wait("v2t", mv)
            for h in (2 * q, 2 * q + 1):
                v.op("tensor_scalar", out=PHI[:, h, :], in0=APR[:, h, :],
                     scalar1=SFT[:, su, h:h + 1], scalar2=EPS16,
                     op0=ALU.mult, op1=ALU.add)

        pst[("exps", su, p)] = val("s2v")
        stage_reduce(p)
        if p > 0:
            stage_rest(p - 1)
        if p == 7:
            stage_rest(7)
            st_["apr_free_v2t"] = val("v2t")
        return val("v2t")

    def emit_phi_vec(su):
        return val("v2t")

    def emit_kv(su, h0, h1, phi_ready_su):
        firstk = (su == 0)
        t.wait("v2t", phi_ready_su)
        if h0 == 0:
            t.wait("s2v", st_["vaug_ready_kv"])
            if su == 0:
                t.wait("v2t", st_["kv_banks_free"])
                t.wait("v2t", st_["nsq_evacd"])
                t.wait("v2t", st_["nsq_bank_free"])
        PHI = PHIB[su % 2]
        for h in range(h0, h1):
            for c in range(2):
                idx = h * 2 + c
                if idx < 28:
                    bank, slot = KVB[idx // 7], idx % 7
                else:
                    bank, slot = pb[2], idx - 28
                stop_kv = (su == 3 and (slot == 6 or idx == 31))
                t.op("matmul", bank[:, slot * 65:slot * 65 + 65],
                     PHI[:, h, c * 128:(c + 1) * 128], VAUG[:, h, 0:65],
                     start=(firstk and slot == 0), stop=stop_kv,
                     inc=[("t2v", 1)] if (stop_kv or idx == 31) else ())
        if h1 == H:
            st_["phi_free2"][su % 2] = val("t2v")

    st_["vaug_ready_kv"] = 0
    for tile in range(NT):
        load_xb(tile, "p1")
        proj_nsq(tile, "p1", ldw_p1)
        prev_phi_ready = st_.get("p1_phi_ready", 0)
        for su in range(4):
            buf = tile % 2
            for p in range(8):
                ph = emit_xw_pair(su, p)
                if su > 0 and p < 4:
                    emit_kv(su - 1, p * 4, (p + 1) * 4, prev_phi_ready)
            cur_phi_ready = ph
            for gr in range(2):
                bank = gr
                t.wait("s2v", max(st_["xw_free"][bank], st_["kproj_free"][bank]))
                for d in range(8):
                    lastd = (d == 7)
                    t.op("matmul", pb[bank][:, :],
                         XB[:, buf, d, su * 128:(su + 1) * 128],
                         WB[:, d, gr * 512:(gr + 1) * 512],
                         start=(d == 0), stop=lastd,
                         inc=[("t2v", 1)] if lastd else ())
                ev = val("t2v")
                if su == 3 and gr == 1:
                    st_["xb_rel"][("p1", tile)] = ("t2v", ev)
                s.wait("t2v", ev)
                s.wait("t2v", st_["phi_free2"][(su - 1) % 2])
                if tile == 0 and su == 0 and gr == 0:
                    s.wait("v2t", vaug_ones)
                s.op("copy", out=VAUG[:, gr * 8:(gr + 1) * 8, 0:64],
                     in_=pb[bank].rearrange("p (h e) -> p h e", h=8)[:, :, :])
                st_["kproj_free"][bank] = val("s2v")
            st_["vaug_ready_kv"] = val("s2v")
            prev_phi_ready = cur_phi_ready
            st_["p1_phi_ready"] = prev_phi_ready
        emit_kv(3, 0, H, prev_phi_ready)
        kv_done = val("t2v")
        v.wait("t2v", kv_done)
        v.wait("v2t", st_["kvacc_done"])
        for b in range(4):
            v.op("tensor_add", KVACC[:, b * 455:(b + 1) * 455],
                 KVACC[:, b * 455:(b + 1) * 455], KVB[b][:, 0:455])
        v.op("tensor_add", KVACC[:, 1820:2080], KVACC[:, 1820:2080], pb[2][:, 0:260])
        st_["kvacc_done"] = val("v2t")
        st_["kv_banks_free"] = val("v2t")
        st_["nsq_bank_free"] = val("v2t")

    vsync()
    v.op("tensor_copy", KVR[:, :, :, 0:65],
         KVACC.rearrange("p (h c e) -> p h c e", h=16, c=2))
    v.op("memset", KVR[:, :, :, 65:66], 0.0)
    st_["kvr_ready"] = val("v2t")
    if debug:
        vsync()
        v.op("tensor_copy", R1.rearrange("p (i j) -> p i j", j=64),
             KVACC.rearrange("p (i j) -> p i j", j=65)[:, 0:16, 0:64])
        vsync()
        sy.wait("v2t", val("v2t"))
        dma(sy, "st", out=dbg["kv"][:, 0:1024], in_=R1[:, :])
        v.wait("st", val("st"))
        v.op("tensor_copy", R1.rearrange("p (i j) -> p i j", j=64),
             KVACC.rearrange("p (i j) -> p i j", j=65)[:, 16:32, 0:64])
        vsync()
        sy.wait("v2t", val("v2t"))
        dma(sy, "st", out=dbg["kv"][:, 1024:2048], in_=R1[:, :])
        v.wait("st", val("st"))
        v.op("tensor_copy", R1[:, 0:32].rearrange("p (i j) -> p i j", j=1),
             KVACC.rearrange("p (i j) -> p i j", j=65)[:, :, 64:65])
        vsync()
        sy.wait("v2t", val("v2t"))
        dma(sy, "st", out=dbg["ksum"][:, :], in_=R1[:, 0:32])
        v.wait("st", val("st"))

    pb6_cm.__exit__(None, None, None)
    pbT2 = nc.psum_tensor("pbT2", [128, 1024], BF16)
    pbT2t = pbT2.__enter__()
    ctx.append(pbT2)
    pbTs = [pbT, pbT2t]

    def phase_reset():
        st_["kproj_free"] = [val("s2v"), val("s2v")]
        st_["xw_free"] = [val("s2v"), val("s2v")]
        st_["nsq_bank_free"] = val("v2t")
        st_["num_free"] = val("v2t")
        st_["y_free"] = val("v2t")
        st_["v_free"] = val("v2t")
        st_["mgt_free"] = val("t2v")
        st_["mrg_free"] = val("t2v")
        st_["pbT_free"] = [("v2t", val("v2t")), ("s2v", val("s2v"))]
        st_["pqt_free"] = val("t2v")
        st_["pqt_free2w"] = [val("t2v"), val("t2v")]
        st_["phi_free"] = val("t2v")
        st_["phi_free2"] = [val("t2v"), val("t2v")]
        st_["u_free"] = [val("s2v"), val("s2v")]
        st_["usb_free"] = [val("t2s"), val("t2s")]
        st_["r1nt_free"] = val("t2s")

    def phase_barrier():
        g.wait("t2v", val("t2v"))
        g.wait("t2s", val("t2s"))
        g.wait("v2t", val("v2t"))
        g.wait("s2v", val("s2v"))

    for half in range(2):
        phase_barrier()
        dma(g, "ldw", out=WA[:, :, :], in_=WqT_h.rearrange("(c p) e -> p c e", p=128))
        dma(g, "ldw", out=WB[:, :, :], in_=WoT_h.rearrange("(c p) e -> p c e", p=128))
        ldw_p2 = val("ldw")
        phase_reset()

        for tile in range(half * 4, half * 4 + 4):
            load_xb(tile, "p2")
            proj_nsq(tile, "p2", ldw_p2)
            pend = []
            for p in range(8):
                st_["p2_phi_ready"] = emit_xw_pair(0, p)
            for su in range(4):
                sub = tile * 4 + su
                rsub = sub % 16
                if sub >= 1:
                    sy.wait("v2t", st_["xtm_free"].get(sub - 1, 0))
                dma(sy, "ldy", out=XTM[:, :],
                    in_=x_tm[sub * 128:(sub + 1) * 128, :])
                xtm_ev = val("ldy")
                phi_ready = st_["p2_phi_ready"]
                pend = list(range(8)) if su < 3 else []

                def drip(k):
                    for _ in range(k):
                        if pend:
                            st_["p2_phi_ready"] = emit_xw_pair(su + 1, pend.pop(0))

                PHI = PHIB[su % 2]
                t.wait("v2t", st_["kvr_ready"])
                wave_trev = {}
                wave_evac = {}
                grp_start_done = [False, False]

                def emit_T(w):
                    bnk = w % 2
                    t.wait("v2t", phi_ready)
                    fl = st_["pbT_free"][bnk]
                    if isinstance(fl, tuple):
                        t.wait(fl[0], fl[1])
                    else:
                        t.wait("v2t", fl)
                    t.wait("v2t", st_["num_free"] if w < 2 else 0)
                    for hh4 in range(4):
                        h = w * 4 + hh4
                        for c in range(2):
                            slot = hh4 * 2 + c
                            t.op("transpose", pbTs[bnk][:, slot * 128:(slot + 1) * 128],
                                 PHI[:, h, c * 128:(c + 1) * 128], IDN[:, :],
                                 inc=[("t2v", 1)])
                    wave_trev[w] = val("t2v")
                    if w == 3:
                        st_["phi_free2"][su % 2] = wave_trev[w]

                def emit_evac(w):
                    bnk = w % 2
                    eng = v if w % 2 == 0 else s
                    lane = "v2t" if w % 2 == 0 else "s2v"
                    eng.wait("t2v", wave_trev[w])
                    eng.wait("t2v", st_["pqt_free2w"][w % 2])
                    if w % 2 == 0:
                        eng.op("tensor_copy", PQT8[:, 0:1024], pbTs[bnk][:, :])
                    else:
                        eng.op("copy", out=PQT8[:, 1024:2048], in_=pbTs[bnk][:, :])
                    st_["pbT_free"][bnk] = (lane, val(lane))
                    wave_evac[w] = (lane, val(lane))

                def emit_num(w):
                    grp = w // 2
                    t.wait(wave_evac[w][0], wave_evac[w][1])
                    if w % 2 == 0:
                        t.wait("v2t", st_["num_free"])
                    for hh4 in range(4):
                        hh = (w % 2) * 4 + hh4
                        h = w * 4 + hh4
                        nb = NUMB[w % 2]
                        nslot = hh4
                        base = (w % 2) * 1024 + hh4 * 256
                        for c in range(2):
                            t.op("matmul", nb[:, nslot * 128:nslot * 128 + 66],
                                 PQT8[:, base + c * 128:base + c * 128 + 128],
                                 KVR[:, h, c, :],
                                 start=(c == 0 and hh4 == 0),
                                 stop=(nslot == 3 and c == 1),
                                 inc=[("t2v", 1)] if c == 1 else ())
                    st_["pqt_free2w"][w % 2] = val("t2v")

                emit_T(0)
                emit_evac(0)
                for w in range(4):
                    if w < 3:
                        emit_T(w + 1)
                        emit_evac(w + 1)
                    emit_num(w)
                    drip(1)
                    if w % 2 == 1:
                        numg = val("t2v")
                        v.wait("t2v", numg)
                        base_h = (w - 1) * 4
                        for bb in range(2):
                            v.op("tensor_scalar",
                                 out=DEN[:, base_h + bb * 4:base_h + bb * 4 + 4],
                                 in0=NUMB[bb].rearrange("p (a b) -> p a b", b=128)[:, :, 64:65]
                                 .rearrange("p a b -> p (a b)"),
                                 scalar1=EPS, scalar2=None, op0=ALU.add)
                        vsync()
                        v.op("reciprocal", RDEN[:, base_h:base_h + 8],
                             DEN[:, base_h:base_h + 8])
                        vsync()
                        v.wait("t2v", st_["mrg_free"])
                        for hh in range(8):
                            h = base_h + hh
                            v.op("tensor_scalar", out=MRG[:, h * 64:(h + 1) * 64],
                                 in0=NUMB[hh // 4][:, (hh % 4) * 128:(hh % 4) * 128 + 64],
                                 scalar1=RDEN[:, h:h + 1], scalar2=None, op0=ALU.mult)
                        st_["num_free"] = val("v2t")
                        drip(1)
                mrg_ready = val("v2t")
                t.wait("v2t", mrg_ready)
                fl = st_["pbT_free"][0]
                t.wait(fl[0], fl[1])
                for c in range(8):
                    t.op("transpose", pbTs[0][:, c * 128:(c + 1) * 128],
                         MRG[:, c * 128:(c + 1) * 128], IDN[:, :],
                         inc=[("t2v", 1)])
                st_["mrg_free"] = val("t2v")
                drip(2)
                s.wait("t2v", st_["mrg_free"])
                s.wait("t2v", st_.get("mgt_wo_done", 0))
                s.op("copy", out=MGT[:, :, :],
                     in_=pbTs[0].rearrange("p (c t) -> p c t", c=8))
                st_["pbT_free"][0] = ("s2v", val("s2v"))
                mgt_ready = val("s2v")
                t.wait("s2v", mgt_ready)
                t.wait("v2t", st_["y_free"])
                for gr in range(2):
                    for ic in range(8):
                        t.op("matmul", YB[gr][:, :], MGT[:, ic, :],
                             WB[:, ic, gr * 512:(gr + 1) * 512],
                             start=(ic == 0), stop=False)
                    t.op("matmul", YB[gr][:, :], ONESR[:, :],
                         BOROW[:, gr * 512:(gr + 1) * 512],
                         start=False, stop=True, inc=[("t2v", 1)])
                st_["mgt_wo_done"] = val("t2v")
                yev = val("t2v")
                drip(8)
                v.wait("t2v", yev)
                v.wait("ldy", xtm_ev)
                vsync()
                for gr in range(2):
                    v.op("scalar_tensor_tensor", out=R1[:, gr * 512:(gr + 1) * 512],
                         in0=YB[gr][:, :], scalar=1.0,
                         in1=XTM[:, gr * 512:(gr + 1) * 512],
                         op0=ALU.mult, op1=ALU.add)
                st_["y_free"] = val("v2t")
                st_["xtm_free"][sub] = val("v2t")
                vsync()
                v.op("reduce_sum", MU[:, 0:1], R1[:, :], axis=AX.X)
                vsync()
                v.op("tensor_scalar", out=MU[:, 2:3], in0=MU[:, 0:1], scalar1=1.0 / D,
                     scalar2=None, op0=ALU.mult)
                vsync()
                v.op("scalar_tensor_tensor", out=JNK[:, :], in0=R1[:, :],
                     scalar=MU[:, 2:3], in1=R1[:, :], op0=ALU.subtract, op1=ALU.mult,
                     accum_out=MU[:, 1:2])
                vsync()
                v.op("tensor_scalar", out=MU[:, 3:4], in0=MU[:, 1:2], scalar1=1.0 / D,
                     scalar2=LNEPS, op0=ALU.mult, op1=ALU.add)
                var_ev = val("v2t")
                s.wait("v2t", var_ev)
                s.op("activation", out=MU[:, 4:5], in_=MU[:, 3:4], func=AF.Ln)
                s.wait("s2v", val("s2v"))
                s.op("activation", out=MU[:, 5:6], in_=MU[:, 4:5], func=AF.Exp,
                     scale=-0.5)
                sq_ev = val("s2v")
                v.wait("s2v", sq_ev)
                v.op("tensor_scalar", out=r1n[:, rsub, :], in0=R1[:, :],
                     scalar1=MU[:, 2:3], scalar2=MU[:, 5:6],
                     op0=ALU.subtract, op1=ALU.mult)

        if debug:
            for rsub in range(16):
                sub = half * 16 + rsub
                v.wait("st", val("st"))
                vsync()
                v.op("tensor_copy", R1[:, :], r1n[:, rsub, :])
                vsync()
                sy.wait("v2t", val("v2t"))
                dma(sy, "st", out=dbg["r1n"][sub * 128:(sub + 1) * 128, :],
                    in_=R1[:, :])
            v.wait("st", val("st"))

        phase_barrier()
        ldw_w1q = []
        ldw_w2q = []
        for q in range(4):
            dma(g, "ldw", out=W1V[:, :, q * 1024:(q + 1) * 1024],
                in_=W1_h.rearrange("(c p) e -> p c e", p=128)[:, :, q * 1024:(q + 1) * 1024])
            ldw_w1q.append(val("ldw"))
            dma(g, "ldw", out=W2V[:, q * 8:(q + 1) * 8, :],
                in_=W2_h.rearrange("(c p) e -> p c e", p=128)[:, q * 8:(q + 1) * 8, :])
            ldw_w2q.append(val("ldw"))
        ldw_p3 = val("ldw")
        phase_reset()

        for tile in range(half * 4, half * 4 + 4):
            for pr in range(2):
                t.wait("v2t", st_["v_free"])
                t.wait("t2s", st_["r1nt_free"])
                for s2 in range(2):
                    sub = tile * 4 + pr * 2 + s2
                    rsub = sub % 16
                    fl = st_["pbT_free"][s2]
                    t.wait(fl[0], fl[1])
                    for c in range(8):
                        t.op("transpose", pbTs[s2][:, c * 128:(c + 1) * 128],
                             r1n[:, rsub, c * 128:(c + 1) * 128], IDN[:, :],
                             inc=[("t2v", 1)])
                    ev = val("t2v")
                    v.wait("t2v", ev)
                    v.wait("t2s", st_["r1nt_free"])
                    v.op("tensor_copy", R1NT[:, :, s2 * 128:(s2 + 1) * 128],
                         pbTs[s2].rearrange("p (c t) -> p c t", c=8))
                    st_["pbT_free"][s2] = ("v2t", val("v2t"))
                r1nt_ready = val("v2t")
                t.wait("v2t", r1nt_ready)

                def emit_u(ic):
                    ub = ic % 2
                    t.wait("ldw", ldw_w1q[ic // 8])
                    t.wait("s2v", st_["u_free"][ub])
                    for d in range(8):
                        t.op("matmul", pb[ub][:, 0:256],
                             W1V[:, d, ic * 128:(ic + 1) * 128], R1NT[:, d, :],
                             start=(d == 0), stop=(d == 7),
                             inc=[("t2s", 1)] if d == 7 else ())
                    uev = val("t2s")
                    if ic == 31:
                        st_["r1nt_free"] = uev
                    s.wait("t2s", uev)
                    s.wait("t2s", st_["usb_free"][ub])
                    s.op("activation", out=USB[:, ub, :], in_=pb[ub][:, 0:256],
                         func=AF.Relu, bias=B1COL[:, ic:ic + 1])
                    st_["u_free"][ub] = val("s2v")

                def emit_v(ic):
                    ub = ic % 2
                    t.wait("ldw", ldw_w2q[ic // 8])
                    t.wait("s2v", st_["u_free"][ub])
                    for s2 in range(2):
                        for gr in range(2):
                            vb = VB[s2 * 2 + gr]
                            t.op("matmul", vb[:, :], USB[:, ub, s2 * 128:(s2 + 1) * 128],
                                 W2V[:, ic, gr * 512:(gr + 1) * 512],
                                 start=(ic == 0), stop=False,
                                 inc=[("t2s", 1)] if (s2 == 1 and gr == 1) else ())
                    st_["usb_free"][ub] = val("t2s")

                emit_u(0)
                for ic in range(1, 32):
                    emit_u(ic)
                    emit_v(ic - 1)
                emit_v(31)
                for s2 in range(2):
                    for gr in range(2):
                        vb = VB[s2 * 2 + gr]
                        t.op("matmul", vb[:, :], ONESR[:, :],
                             B2ROW[:, gr * 512:(gr + 1) * 512],
                             start=False, stop=True, inc=[("t2v", 1)])
                vev = val("t2v")
                v.wait("t2v", vev)
                for s2 in range(2):
                    sub = tile * 4 + pr * 2 + s2
                    rsub = sub % 16
                    ob = sub % 2
                    v.wait("st", st_["outs_free"].get(sub - 2, 0))
                    vsync()
                    for gr in range(2):
                        v.op("scalar_tensor_tensor", out=R1[:, gr * 512:(gr + 1) * 512],
                             in0=VB[s2 * 2 + gr][:, :], scalar=1.0,
                             in1=r1n[:, rsub, gr * 512:(gr + 1) * 512],
                             op0=ALU.mult, op1=ALU.add)
                    if s2 == 1:
                        st_["v_free"] = val("v2t")
                    vsync()
                    v.op("reduce_sum", MU[:, 0:1], R1[:, :], axis=AX.X)
                    vsync()
                    v.op("tensor_scalar", out=MU[:, 2:3], in0=MU[:, 0:1],
                         scalar1=1.0 / D, scalar2=None, op0=ALU.mult)
                    vsync()
                    v.op("scalar_tensor_tensor", out=OUTS[:, ob, :], in0=R1[:, :],
                         scalar=MU[:, 2:3], in1=R1[:, :],
                         op0=ALU.subtract, op1=ALU.mult, accum_out=MU[:, 1:2])
                    vsync()
                    v.op("tensor_scalar", out=MU[:, 3:4], in0=MU[:, 1:2],
                         scalar1=1.0 / D, scalar2=LNEPS, op0=ALU.mult, op1=ALU.add)
                    var_ev = val("v2t")
                    s.wait("v2t", var_ev)
                    s.op("activation", out=MU[:, 4:5], in_=MU[:, 3:4], func=AF.Ln)
                    s.wait("s2v", val("s2v"))
                    s.op("activation", out=MU[:, 5:6], in_=MU[:, 4:5], func=AF.Exp,
                         scale=-0.5)
                    sq_ev = val("s2v")
                    v.wait("s2v", sq_ev)
                    v.op("tensor_scalar", out=OUTS[:, ob, :], in0=R1[:, :],
                         scalar1=MU[:, 2:3], scalar2=MU[:, 5:6],
                         op0=ALU.subtract, op1=ALU.mult)
                    oev = val("v2t")
                    sy.wait("v2t", oev)
                    dma(sy, "st", out=out_h[sub * 128:(sub + 1) * 128, :],
                        in_=OUTS[:, ob, :])
                    st_["outs_free"][sub] = val("st")

    sy.wait("st", val("st"))
    g.wait("ldw", val("ldw"))
    g.wait("ldx", val("ldx"))

    blk_cm = nc.Block()
    blk = blk_cm.__enter__()

    def emitter(ename):
        def _(eng):
            for f in ops[ename]:
                f(eng)
        return _
    blk.tensor(emitter("tensor"))
    blk.vector(emitter("vector"))
    blk.scalar(emitter("scalar"))
    blk.sync(emitter("sync"))
    blk.gpsimd(emitter("gpsimd"))
    blk_cm.__exit__(None, None, None)
    for h in reversed(ctx):
        h.__exit__(None, None, None)
    return nc


def make_inputs(x_b, Wq, Wk, Wv, Wo, bo, proj, W1, b1, W2, b2):
    c = np.ascontiguousarray
    ind = np.zeros((8, 128, H), np.float32)
    for ch in range(8):
        ind[ch, 0:64, 2 * ch] = 1.0
        ind[ch, 64:128, 2 * ch + 1] = 1.0
    pt = np.asarray(proj).transpose(0, 2, 1)
    projBD = np.zeros((8, 128, 512), np.float32)
    projBD[:, 0:64, 0:256] = pt[0::2]
    projBD[:, 64:128, 256:512] = pt[1::2]
    return {
        "xT": c(np.asarray(x_b).T),
        "x": c(np.asarray(x_b)),
        "WqT": c(np.asarray(Wq).T), "WkT": c(np.asarray(Wk).T),
        "WvT": c(np.asarray(Wv).T), "WoT": c(np.asarray(Wo).T),
        "W1": c(np.asarray(W1)), "W2": c(np.asarray(W2)),
        "projBD": c(projBD),
        "ind": ind,
        "ident": np.eye(128, dtype=np.float32),
        "onesrow": np.ones((1, 128), np.float32),
        "onescol": np.ones((128, 1), np.float32),
        "borow": c(np.asarray(bo).reshape(1, D)),
        "b2row": c(np.asarray(b2).reshape(1, D)),
        "b1col": c(np.asarray(b1).reshape(32, 128).T),
    }



from concourse.bass_utils import run_bass_kernel_spmd


def kernel(L0_attn_x, L0_attn_Wq, L0_attn_Wk, L0_attn_Wv, L0_attn_Wo, L0_attn_bo,
           L0_attn_proj, L0_ffn_W1, L0_ffn_b1, L0_ffn_W2, L0_ffn_b2):
    x = np.asarray(L0_attn_x)
    nc = build_program(debug=False)
    in_maps = []
    for b in range(8):
        in_maps.append(make_inputs(x[b], L0_attn_Wq, L0_attn_Wk, L0_attn_Wv,
                                   L0_attn_Wo, L0_attn_bo, L0_attn_proj,
                                   L0_ffn_W1, L0_ffn_b1, L0_ffn_W2, L0_ffn_b2))
    res = run_bass_kernel_spmd(nc, in_maps, core_ids=list(range(8)))
    out = np.stack([np.asarray(res.results[b]["out"]) for b in range(8)], axis=0)
    return out.astype(np.float32)
